# version 08
# baseline (speedup 1.0000x reference)
import contextlib
import numpy as np
import ml_dtypes

import concourse.bass as bass
import concourse.mybir as mybir
from concourse.bass_utils import run_bass_kernel_spmd

F32 = mybir.dt.float32
BF16 = mybir.dt.bfloat16
I32 = mybir.dt.int32
ALU = mybir.AluOpType
ACTF = mybir.ActivationFunctionType
BF = ml_dtypes.bfloat16
AXX = mybir.AxisListType.X

D = 1024
T = 4096
C = 512
NCH = T // C
NDC = D // 128
ATM_OFF = (0, 512, 896, 1152)
ATM_W = (512, 384, 256, 128)

ENGINES = ("sync", "gpsimd", "tensor", "vector", "scalar")
ESEM = {"tensor": "e_t", "vector": "e_v", "scalar": "e_s", "gpsimd": "e_g",
        "sync": "e_y"}


class Buf:
    __slots__ = ("ap", "wtok", "rtoks")

    def __init__(self, ap):
        self.ap = ap
        self.wtok = None
        self.rtoks = []

    def __getitem__(self, key):
        return self.ap[key]


class Bank(Buf):
    __slots__ = ("name",)

    def __init__(self, ap, name):
        super().__init__(ap)
        self.name = name


class K:
    def __init__(self, nc):
        self.nc = nc
        self.es = contextlib.ExitStack()
        self.ops = {e: [] for e in ENGINES}
        self.cnt = {}
        for s in ("si_x", "si_w", "si_m", "si_sc", "si_out", "sg", "sg_su",
                  "sg_tap", "e_t", "e_v", "e_s", "e_g", "e_y"):
            self.cnt[s] = 0
        self._win = []

    def sb(self, name, shape, dtype, es=None):
        h = (es or self.es).enter_context(
            self.nc.sbuf_tensor(name, list(shape), dtype))
        return Buf(h[:])

    @staticmethod
    def views(buf, n, w):
        return [Buf(buf.ap[:, w * i:w * (i + 1)]) for i in range(n)]

    def bank(self, name, dtype=F32, cols=None):
        if cols is None:
            cols = 512 if dtype == F32 else 1024
        h = self.es.enter_context(self.nc.psum_tensor(name, [128, cols], dtype))
        return Bank(h[:], name)

    paranoid = 0

    def _emit(self, eng, fn, waits, inc, d):
        if self.paranoid == 1:
            waits = list(waits) + [(s, v) for s, v in self.cnt.items() if v > 0]
        elif self.paranoid == 2:
            waits = list(waits) + [(s, v) for s, v in self.cnt.items()
                                   if v > 0 and s.startswith("e_")]
        elif self.paranoid == 3:
            waits = list(waits) + [(s, v) for s, v in self.cnt.items()
                                   if v > 0 and not s.startswith("e_")]
        self.cnt[inc] += d
        self.ops[eng].append((fn, tuple(waits), (inc, d)))
        return (inc, self.cnt[inc])

    @staticmethod
    def _waits(ins):
        ws = []
        for o in ins:
            if o is None:
                continue
            if isinstance(o, tuple):
                ws.append(o)
            elif o.wtok is not None:
                ws.append(o.wtok)
        return ws

    def _do(self, eng, fn, ins, out, sem, d, waw=True):
        waits = self._waits(ins)
        if out is not None:
            if waw and out.wtok is not None:
                waits.append(out.wtok)
            waits += out.rtoks
        waits = [(s, self.cnt[s]) if not s.startswith("e_") else (s, v)
                 for (s, v) in waits]
        tok = self._emit(eng, fn, waits, sem, d)
        for o in ins:
            if o is not None and not isinstance(o, tuple):
                o.rtoks.append(tok)
        if out is not None:
            out.wtok = tok
            out.rtoks = []
        return tok

    def op(self, eng, fn, ins=(), out=None):
        return self._do(eng, fn, ins, out, ESEM[eng], 1)

    def dma(self, eng, sem, fn, ins=(), out=None, waw=True):
        return self._do(eng, fn, ins, out, sem, 16, waw=waw)

    def mm(self, bank, out_ap, lhsT, rhs, start, stop, ins=(),
           is_transpose=False):
        base = lhsT.base_partition()
        lo, hi = base // 32, (base + lhsT.shape[0] + 31) // 32
        if hi - lo >= 4:
            self._win = []
        else:
            for (bn, l2, h2) in self._win:
                if bn == bank.name and (h2 <= lo or hi <= l2):
                    raise RuntimeError(
                        f"strip conflict bank={bank.name} ({l2},{h2})vs({lo},{hi})")
            self._win = [w for w in self._win if w[0] != bank.name] + \
                [(bank.name, lo, hi)]
        ins = list(ins)
        if start:
            ins += bank.rtoks
            bank.rtoks = []
        kw = {"is_transpose": True} if is_transpose else {}
        tok = self.op("tensor",
                      lambda t, o=out_ap, a=lhsT, b=rhs, s1=start, s2=stop, kw=kw:
                      t.matmul(o, a, b, start=s1, stop=s2, **kw),
                      ins=ins)
        bank.wtok = tok
        return tok

    def finalize(self, final_waits):
        nc = self.nc
        referenced = {}
        for ename in ENGINES:
            for _, waits, _ in self.ops[ename]:
                for (s, v) in waits:
                    referenced.setdefault(s, set()).add(v)
        for (s, v) in final_waits:
            referenced.setdefault(s, set()).add(v)
        remap = {}
        newcnt = {}
        for s in self.cnt:
            if not s.startswith("e_"):
                continue
            keep = sorted(referenced.get(s, ()))
            remap[s] = {v: i + 1 for i, v in enumerate(keep)}
            newcnt[s] = len(keep)

        def map_wait(s, v):
            if s in remap:
                return (s, remap[s][v])
            return (s, v)

        sems = {}
        with contextlib.ExitStack() as st:
            for name in self.cnt:
                sems[name] = st.enter_context(nc.semaphore(name))
            block = st.enter_context(nc.Block())

            def mk(ename):
                def body(eng):
                    hw = {}
                    run = {s: 0 for s in self.cnt}
                    for fn, waits, (isem, d) in self.ops[ename]:
                        for (s0, v0) in waits:
                            s, v = map_wait(s0, v0)
                            if hw.get(s, 0) < v:
                                eng.wait_ge(sems[s], v)
                                hw[s] = v
                        inst = fn(eng)
                        if isem in remap:
                            run[isem] += d
                            nv = remap[isem].get(run[isem])
                            if nv is not None:
                                inst.then_inc(sems[isem], nv - remap[isem].get(
                                    max((x for x in remap[isem] if x < run[isem]),
                                        default=0), 0) if False else 1)
                        else:
                            inst.then_inc(sems[isem], d)
                    if ename == "sync":
                        for (s0, v0) in final_waits:
                            s, v = map_wait(s0, v0)
                            eng.wait_ge(sems[s], v)
                return body

            for ename in ENGINES:
                getattr(block, ename)(mk(ename))


def build_program(taps=(), paranoid=0):
    nc = bass.Bass("TRN2", target_bir_lowering=False)
    k = K(nc)
    k.paranoid = paranoid
    ap = {}

    def inp(name, shape, dtype):
        ap[name] = nc.dram_tensor(name, list(shape), dtype,
                                  kind="ExternalInput").ap()

    inp("xT", [D, T], F32)
    inp("wqkv", [D, 3 * D], F32)
    inp("wout", [D, D], F32)
    inp("bq_row", [1, D], F32)
    inp("bk_row", [1, D], F32)
    inp("bv_row", [1, D], F32)
    inp("bout_fm", [128, NDC], F32)
    inp("omgblk", [128, 8 * 128], F32)
    inp("qn2", [2, 1], F32)
    inp("qw2", [2, 1], F32)
    inp("qT", [6 * D, D], I32)
    inp("scT", [6 * D, D], F32)
    inp("bcol_fm", [128, 48], F32)
    inp("gam_fm", [128, 24], F32)
    inp("bet_fm", [128, 24], F32)
    inp("laT", [D, 192], F32)
    inp("lbT", [64, 3 * D], F32)
    inp("ident", [128, 128], BF16)
    inp("diagmask", [128, 128], BF16)
    inp("selhalf", [128, 2], BF16)
    inp("selpair_f", [2, 128], F32)
    inp("selrq_f", [2, 128], F32)
    inp("ones_1x128_b", [1, 128], BF16)
    inp("nones_1x128_b", [1, 128], BF16)
    inp("ones_1x128_f", [1, 128], F32)
    inp("ones_c128_b", [128, 1], BF16)
    inp("ones_c128_f", [128, 1], F32)
    inp("ones_row_b", [1, C], BF16)
    inp("onescol3", [128, 3], BF16)

    xn_scr = nc.dram_tensor("xn_scr", [D, T], F32, kind="Internal").ap()
    s_scr = nc.dram_tensor("s_scr", [D, T], F32, kind="Internal").ap()
    w_scr = nc.dram_tensor("w_scr", [6 * D, D], BF16, kind="Internal").ap()
    outT = nc.dram_tensor("outT", [D, T], F32, kind="ExternalOutput").ap()
    tap_ap = {}
    for tname, tshape in [("qf0", [128, NDC * C]), ("kf0", [128, NDC * C]),
                          ("v0", [128, 4 * 1040]), ("attn0", [128, NDC * C]),
                          ("y0", [128, NDC * C]), ("g10", [128, NDC * C]),
                          ("h1", [128, NDC * C]), ("u0", [64, C]),
                          ("mu0", [1, C]), ("rs0", [1, C]),
                          ("h2", [128, NDC * C]), ("h3", [128, NDC * C]),
                          ("gg0", [128, C]), ("y1", [128, NDC * C]),
                          ("g11", [128, NDC * C])]:
        if tname in taps:
            tap_ap[tname] = nc.dram_tensor(tname, tshape, F32,
                                           kind="ExternalOutput").ap()

    def tap(tname, buf, cond=True):
        if tname in tap_ap and cond:
            k.dma("gpsimd", "sg_tap",
                  lambda e, d=tap_ap[tname], s=buf.ap: e.dma_start(out=d, in_=s),
                  ins=[buf])

    B = [k.bank(f"pb{i}") for i in range(7)]
    PT = k.bank("pbt", BF16)
    es1 = contextlib.ExitStack()

    ident_sb = k.sb("ident_sb", [128, 128], BF16)
    selhalf_sb = k.sb("selhalf_sb", [128, 2], BF16)
    selpair_sb = k.sb("selpair_sb", [2, 128], F32)
    selrq_sb = k.sb("selrq_sb", [2, 128], F32)
    ones1x128b = k.sb("ones1x128b", [1, 128], BF16)
    nones1x128b = k.sb("nones1x128b", [1, 128], BF16)
    ones1x128f = k.sb("ones1x128f", [1, 128], F32)
    onesc128b = k.sb("onesc128b", [128, 1], BF16)
    onesc128f = k.sb("onesc128f", [128, 1], F32)
    onesrowb = k.sb("onesrowb", [1, C], BF16)
    onescol3 = k.sb("onescol3_sb", [128, 3], BF16)
    bout_sb = k.sb("bout_sb", [128, NDC], F32)
    bcol_sb = k.sb("bcol_sb", [128, 48], F32)
    gam_sb = k.sb("gam_sb", [128, 24], F32)
    bet_sb = k.sb("bet_sb", [128, 24], F32)
    laT_sb = k.sb("laT_sb", [128, NDC * 192], BF16)
    lbT_sb = k.sb("lbT_sb", [64, 3 * D], BF16)
    wbar_sb = k.sb("wbar_sb", [128, 48], BF16)
    lbar_f = k.sb("lbar_f", [64, 3], F32)
    lbar_sb = k.sb("lbar_sb", [64, 3], BF16)
    bsum_col = k.sb("bsum_col", [128, 6], F32)
    bbar_sb = k.sb("bbar_sb", [1, 6], BF16)
    hi_col = k.sb("hi_col", [128, 1], F32)
    lo_col = k.sb("lo_col", [128, 1], F32)
    lncs_col = k.sb("lncs_col", [128, 1], F32)
    scol_sb = k.sb("scol_sb", [128, 1], F32)
    qn_sb = k.sb("qn_sb", [2, 1], F32)
    qw_sb = k.sb("qw_sb", [2, 1], F32)
    su = [k.sb(f"su{i}", [2, 1], F32) for i in range(4)]
    st_var = k.sb("st_var", [1, C], F32)
    st_ln = k.sb("st_ln", [1, C], F32)
    st_rs = k.sb("st_rs", [1, C], F32)

    wqkv_sb = k.sb("wqkv_sb", [128, NDC * 3 * D], BF16, es=es1)
    wout_sb = k.sb("wout_sb", [128, NDC * D], BF16, es=es1)
    omg_sb = k.sb("omg_sb", [128, 8 * 128], BF16, es=es1)
    dmask_sb = k.sb("dmask_sb", [128, 128], BF16, es=es1)
    bqkv_row = k.sb("bqkv_row", [1, 2 * D], BF16, es=es1)
    bv_row_sb = k.sb("bv_row_sb", [1, D], BF16, es=es1)
    s2s_row = k.sb("s2s_row", [1, 128], F32, es=es1)
    s2s_2 = k.sb("s2s_2", [2, 128], F32, es=es1)
    sel2s = k.sb("sel2s", [2, 128], F32, es=es1)
    S_sb = [k.sb(f"S_sb{p}", [128, 8 * 65], F32, es=es1) for p in range(2)]
    SZ_bf = k.sb("SZ_bf", [128, 65], BF16, es=es1)
    zq_sb = k.sb("zq_sb", [128, 4], BF16, es=es1)
    def ldc(dst_buf, src_ap, dst_ap=None):
        da = dst_ap if dst_ap is not None else dst_buf.ap
        return k.dma("sync", "si_m",
                     lambda e, d=da, s=src_ap: e.dma_start(out=d, in_=s),
                     out=dst_buf)

    def ldg(dst_buf, src_ap, dst_ap=None, sem="sg_su"):
        da = dst_ap if dst_ap is not None else dst_buf.ap
        return k.dma("gpsimd", sem,
                     lambda e, d=da, s=src_ap: e.dma_start(out=d, in_=s),
                     out=dst_buf)

    for b_, nm in [(ident_sb, "ident"), (dmask_sb, "diagmask"),
                   (selhalf_sb, "selhalf"), (selpair_sb, "selpair_f"),
                   (selrq_sb, "selrq_f"), (ones1x128b, "ones_1x128_b"),
                   (nones1x128b, "nones_1x128_b"), (ones1x128f, "ones_1x128_f"),
                   (onesc128b, "ones_c128_b"), (onesc128f, "ones_c128_f"),
                   (onesrowb, "ones_row_b"), (onescol3, "onescol3"),
                   (bout_sb, "bout_fm"), (bcol_sb, "bcol_fm"),
                   (gam_sb, "gam_fm"), (bet_sb, "bet_fm"),
                   (qn_sb, "qn2"), (qw_sb, "qw2")]:
        ldc(b_, ap[nm])
    ldg(omg_sb, ap["omgblk"])
    ldg(bqkv_row, ap["bq_row"], dst_ap=bqkv_row[0:1, 0:D])
    ldg(bqkv_row, ap["bk_row"], dst_ap=bqkv_row[0:1, D:2 * D])
    ldg(bv_row_sb, ap["bv_row"])
    for dc in range(NDC):
        r = slice(128 * dc, 128 * dc + 128)
        ldg(wqkv_sb, ap["wqkv"][r, :],
            dst_ap=wqkv_sb[:, 3 * D * dc:3 * D * (dc + 1)])
        ldg(wout_sb, ap["wout"][r, :], dst_ap=wout_sb[:, D * dc:D * (dc + 1)])
        ldg(laT_sb, ap["laT"][r, :], dst_ap=laT_sb[:, 192 * dc:192 * (dc + 1)])
    ldg(lbT_sb, ap["lbT"])

    k.op("vector", lambda v: v.tensor_scalar(
        out=su[0][0:2, :], in0=qn_sb[0:2, :], scalar1=0.0, scalar2=2.0,
        op0=ALU.max, op1=ALU.mult), ins=[qn_sb], out=su[0])
    k.op("scalar", lambda s: s.activation(out=su[1][0:2, :], in_=su[0][0:2, :],
                                          func=ACTF.Ln), ins=[su[0]], out=su[1])
    k.op("scalar", lambda s: s.activation(out=su[2][0:2, :], in_=su[1][0:2, :],
                                          func=ACTF.Exp, scale=0.5),
         ins=[su[1]], out=su[2])
    k.mm(B[0], B[0].ap[0:1, 0:128], su[2].ap[0:2, 0:1], selrq_sb.ap, True, True,
         ins=[su[2], selrq_sb])
    tok = k.op("vector", lambda v: v.tensor_copy(out=s2s_row.ap,
                                                 in_=B[0].ap[0:1, 0:128]),
               ins=[B[0]], out=s2s_row)
    B[0].rtoks.append(tok)
    k.mm(B[1], B[1].ap[0:2, 0:128], ones1x128f.ap[0:1, 0:2], s2s_row.ap,
         True, True, ins=[ones1x128f, s2s_row])
    tok = k.op("vector", lambda v: v.tensor_copy(out=s2s_2.ap,
                                                 in_=B[1].ap[0:2, 0:128]),
               ins=[B[1]], out=s2s_2)
    B[1].rtoks.append(tok)
    k.op("vector", lambda v: v.tensor_tensor(
        out=sel2s.ap, in0=selpair_sb.ap, in1=s2s_2.ap, op=ALU.mult),
        ins=[selpair_sb, s2s_2], out=sel2s)
    k.mm(B[2], B[2].ap[:, 0:1], selrq_sb.ap, qn_sb.ap[0:2, :], True, True,
         ins=[selrq_sb, qn_sb])
    tok = k.op("vector", lambda v: v.tensor_copy(out=scol_sb.ap,
                                                 in_=B[2].ap[:, 0:1]),
               ins=[B[2]], out=scol_sb)
    B[2].rtoks.append(tok)
    k.op("vector", lambda v: v.tensor_scalar(
        out=hi_col.ap, in0=scol_sb.ap, scalar1=20.0, scalar2=None, op0=ALU.add),
        ins=[scol_sb], out=hi_col)
    k.op("vector", lambda v: v.tensor_scalar(
        out=lo_col.ap, in0=scol_sb.ap, scalar1=-20.0, scalar2=None, op0=ALU.add),
        ins=[scol_sb], out=lo_col)
    k.op("vector", lambda v: v.tensor_scalar(
        out=su[3][0:2, :], in0=qw_sb[0:2, :], scalar1=0.0, scalar2=None,
        op0=ALU.max), ins=[qw_sb], out=su[3])
    k.op("scalar", lambda s: s.activation(out=su[0][0:2, :], in_=su[3][0:2, :],
                                          func=ACTF.Ln), ins=[su[3]], out=su[0])
    k.op("vector", lambda v: v.tensor_scalar(
        out=su[1][0:2, :], in0=su[0][0:2, :], scalar1=0.5,
        scalar2=-0.5 * float(np.log(32.0)), op0=ALU.mult, op1=ALU.add),
        ins=[su[0]], out=su[1])
    k.mm(B[3], B[3].ap[:, 0:1], selrq_sb.ap, su[1].ap[0:2, :], True, True,
         ins=[selrq_sb, su[1]])
    tok = k.op("vector", lambda v: v.scalar_tensor_tensor(
        out=lncs_col.ap, in0=B[3].ap[:, 0:1], scalar=1.0, in1=scol_sb.ap,
        op0=ALU.bypass, op1=ALU.subtract), ins=[B[3], scol_sb], out=lncs_col)
    B[3].rtoks.append(tok)
    for l in range(6):
        k.op("vector", lambda v, l=l: v.tensor_reduce(
            out=bsum_col[:, l:l + 1], in_=bcol_sb[:, 8 * l:8 * l + 8],
            axis=AXX, op=ALU.add), ins=[bcol_sb], out=bsum_col)
    k.mm(B[4], B[4].ap[0:1, 0:6], onesc128f.ap, bsum_col.ap, True, True,
         ins=[onesc128f, bsum_col])
    tok = k.op("vector", lambda v: v.tensor_copy(out=bbar_sb.ap,
                                                 in_=B[4].ap[0:1, 0:6]),
               ins=[B[4]], out=bbar_sb)
    B[4].rtoks.append(tok)
    for b_ in range(3):
        k.op("vector", lambda v, b_=b_: v.tensor_reduce(
            out=lbar_f[0:64, b_:b_ + 1], in_=lbT_sb[0:64, D * b_:D * (b_ + 1)],
            axis=AXX, op=ALU.add), ins=[lbT_sb], out=lbar_f)
    k.op("vector", lambda v: v.tensor_copy(out=lbar_sb.ap, in_=lbar_f.ap),
         ins=[lbar_f], out=lbar_sb)

    x_sb = k.sb("x_sb", [128, NDC * C], F32, es=es1)
    xb_buf = [k.sb("xb0", [128, C], BF16, es=es1)] * 2
    sqx_buf = [k.sb("sqx0", [128, C], BF16, es=es1)] * 2
    t1_buf = [k.sb("t1_0", [128, C], F32, es=es1)] * 2
    xnf_buf = [k.sb(f"xnf{i}", [128, C], F32, es=es1) for i in range(2)]
    xn_bf = k.sb("xn_bf", [128, NDC * C], BF16, es=es1)
    xnv = K.views(xn_bf, NDC, C)
    st_mu = k.sb("st_mu", [1, C], F32, es=es1)
    st_msq = k.sb("st_msq", [1, C], F32, es=es1)
    st_nmu = k.sb("st_nmu", [1, C], BF16, es=es1)
    qb_buf = [k.sb(f"qb{i}", [128, C], BF16, es=es1) for i in range(3)]
    sq_buf = [k.sb(f"sq{i}", [128, C], BF16, es=es1) for i in range(2)]
    s2_sb = [k.sb("s2_0", [2, C], F32, es=es1)] * 2
    rln_sb = [k.sb("rln0", [2, C], F32, es=es1)] * 2
    rex_sb = s2_sb
    rmin_sb = rln_sb
    rbs_sb = [k.sb(f"rbs{i}", [128, C], F32, es=es1) for i in range(2)]
    y_buf = [k.sb("yb0", [128, C], F32, es=es1)] * 2
    c2_buf = [k.sb("c2_0", [128, C], F32, es=es1)] * 2
    qf_sb = k.sb("qf_sb", [128, NDC * C], BF16, es=es1)
    kf_sb = k.sb("kf_sb", [128, NDC * C], BF16, es=es1)
    qfv = K.views(qf_sb, NDC, C)
    kfv = K.views(kf_sb, NDC, C)
    attnv = qfv
    v_sb = k.sb("v_sb", [128, 4 * 1040], BF16, es=es1)
    vv = K.views(v_sb, 4, 1040)
    kfT_sb = k.sb("kfT_sb", [128, 512], BF16, es=es1)
    atm_sb = k.sb("atm_sb", [128, 2 * 1280], BF16, es=es1)
    den_sb = k.sb("den_sb", [2, C], F32, es=es1)
    dln_sb = k.sb("dln_sb", [2, C], F32, es=es1)
    recip_sb = den_sb
    ctx_sb = k.sb("ctx_sb", [128, C], F32, es=es1)
    sout_buf = [k.sb(f"sout{i}", [128, C], F32, es=es1) for i in range(2)]
    dq_qi = [k.sb("dqqi0", [128, 512], I32, es=es1)]
    dq_sc = [k.sb("dqsc0", [128, 512], F32, es=es1)]
    dq_t = [k.sb("dqt0", [128, 512], F32, es=es1)]
    dq_w = [k.sb("dqw0", [128, 512], BF16, es=es1)]

    for tt in range(4):
        v65 = vv[tt].ap.rearrange("p (a b) -> p a b", b=65)
        k.op("gpsimd", lambda g, v65=v65: g.memset(v65[:, :, 64:65], 1.0),
             out=vv[tt])
    k.op("gpsimd", lambda g: g.memset(zq_sb[:, 1:3], 0.0), out=zq_sb)
    k.op("gpsimd", lambda g: g.memset(S_sb[0].ap, 0.0), out=S_sb[0])

    for idx in range(96):
        half = idx % 2
        l, dc = divmod(idx // 2, NDC)
        sl = 0
        r = slice(D * l + 128 * dc, D * l + 128 * dc + 128)
        cseg = slice(512 * half, 512 * half + 512)
        ldg(dq_qi[sl], ap["qT"][r, cseg], sem="sg")
        ldg(dq_sc[sl], ap["scT"][r, cseg], sem="sg")
        k.op("gpsimd", lambda g, sl=sl: g.tensor_scalar(
            out=dq_t[sl].ap, in0=dq_qi[sl].ap, scalar1=1.0 / 7.5, scalar2=-1.0,
            op0=ALU.mult, op1=ALU.add), ins=[dq_qi[sl]], out=dq_t[sl])
        k.op("gpsimd", lambda g, sl=sl: g.tensor_tensor(
            out=dq_w[sl].ap, in0=dq_t[sl].ap, in1=dq_sc[sl].ap, op=ALU.mult),
            ins=[dq_t[sl], dq_sc[sl]], out=dq_w[sl])
        k.dma("gpsimd", "sg",
              lambda e, sl=sl, r=r, cseg=cseg: e.dma_start(
                  out=w_scr[r, cseg], in_=dq_w[sl].ap), ins=[dq_w[sl]])


    def features(oc, bank, dstbuf):
        sl = oc % 2
        qb = qb_buf[oc % 3]
        tok = k.op("scalar", lambda s, qb=qb, bank=bank: s.copy(
            out=qb.ap, in_=bank.ap), ins=[bank], out=qb)
        bank.rtoks.append(tok)
        k.op("vector", lambda v, qb=qb, sl=sl: v.tensor_tensor(
            out=sq_buf[sl].ap, in0=qb.ap, in1=qb.ap, op=ALU.mult),
            ins=[qb], out=sq_buf[sl])
        k.mm(B[4], B[4].ap[0:2, :], selhalf_sb.ap, sq_buf[sl].ap,
             True, True, ins=[selhalf_sb, sq_buf[sl]])
        tok = k.op("vector", lambda v, sl=sl: v.tensor_copy(
            out=s2_sb[sl].ap, in_=B[4].ap[0:2, :]), ins=[B[4]], out=s2_sb[sl])
        B[4].rtoks.append(tok)
        k.op("scalar", lambda s, sl=sl: s.activation(
            out=rln_sb[sl].ap, in_=s2_sb[sl].ap, func=ACTF.Ln),
            ins=[s2_sb[sl]], out=rln_sb[sl])
        k.op("scalar", lambda s, sl=sl: s.activation(
            out=rex_sb[sl].ap, in_=rln_sb[sl].ap, func=ACTF.Exp, scale=-0.5),
            ins=[rln_sb[sl]], out=rex_sb[sl])
        k.op("vector", lambda v, sl=sl: v.tensor_scalar(
            out=rmin_sb[sl].ap, in0=rex_sb[sl].ap, scalar1=1e12,
            scalar2=None, op0=ALU.min), ins=[rex_sb[sl]], out=rmin_sb[sl])
        k.mm(B[4], B[4].ap[:, :], sel2s.ap, rmin_sb[sl].ap, True, True,
             ins=[sel2s, rmin_sb[sl]])
        tok = k.op("vector", lambda v, sl=sl: v.tensor_copy(
            out=rbs_sb[sl].ap, in_=B[4].ap), ins=[B[4]], out=rbs_sb[sl])
        B[4].rtoks.append(tok)
        k.mm(bank, bank.ap[:, :], omg_sb[:, 128 * oc:128 * oc + 128],
             qb.ap, True, True, ins=[omg_sb, qb])
        tok = k.op("vector", lambda v, sl=sl, bank=bank: v.scalar_tensor_tensor(
            out=y_buf[sl].ap, in0=rbs_sb[sl].ap, scalar=1.0, in1=bank.ap,
            op0=ALU.bypass, op1=ALU.mult), ins=[rbs_sb[sl], bank],
            out=y_buf[sl])
        bank.rtoks.append(tok)
        k.op("vector", lambda v, sl=sl: v.tensor_scalar(
            out=c2_buf[sl].ap, in0=y_buf[sl].ap, scalar1=hi_col.ap,
            scalar2=lo_col.ap, op0=ALU.min, op1=ALU.max),
            ins=[y_buf[sl], hi_col, lo_col], out=c2_buf[sl])
        k.op("scalar", lambda s, sl=sl, dstbuf=dstbuf: s.activation(
            out=dstbuf.ap, in_=c2_buf[sl].ap,
            func=ACTF.Exp, bias=lncs_col.ap),
            ins=[c2_buf[sl], lncs_col], out=dstbuf)

    for c in range(NCH):
        cs = slice(C * c, C * c + C)
        pcur, pnew = c % 2, 1 - (c % 2)
        Scur, Snew = S_sb[pcur], S_sb[pnew]
        for dc in range(NDC):
            k.dma("sync", "si_x",
                  lambda e, dc=dc, cs=cs: e.dma_start(
                      out=x_sb[:, C * dc:C * dc + C],
                      in_=ap["xT"][128 * dc:128 * dc + 128, cs]),
                  out=x_sb, waw=False)
        for dc in range(NDC):
            sl = dc % 2
            xd = x_sb[:, C * dc:C * dc + C]
            k.op("scalar", lambda s, xd=xd, sl=sl: s.copy(
                out=xb_buf[sl].ap, in_=xd), ins=[x_sb], out=xb_buf[sl])
            k.op("vector", lambda v, xd=xd, sl=sl: v.tensor_tensor(
                out=sqx_buf[sl].ap, in0=xd, in1=xd, op=ALU.mult),
                ins=[x_sb], out=sqx_buf[sl])
            k.mm(B[5], B[5].ap[0:1, :], onesc128b.ap, xb_buf[sl].ap,
                 dc == 0, dc == NDC - 1, ins=[onesc128b, xb_buf[sl]])
            k.mm(B[6], B[6].ap[0:1, :], onesc128b.ap, sqx_buf[sl].ap,
                 dc == 0, dc == NDC - 1, ins=[onesc128b, sqx_buf[sl]])
        tok = k.op("vector", lambda v: v.tensor_scalar(
            out=st_mu.ap, in0=B[5].ap[0:1, :], scalar1=1.0 / 1024.0,
            scalar2=None, op0=ALU.mult), ins=[B[5]], out=st_mu)
        B[5].rtoks.append(tok)
        k.op("vector", lambda v: v.tensor_tensor(
            out=st_msq.ap, in0=st_mu.ap, in1=st_mu.ap, op=ALU.mult),
            ins=[st_mu], out=st_msq)
        tok = k.op("vector", lambda v: v.scalar_tensor_tensor(
            out=st_var.ap, in0=B[6].ap[0:1, :], scalar=1.0 / 1024.0,
            in1=st_msq.ap, op0=ALU.mult, op1=ALU.subtract),
            ins=[B[6], st_msq], out=st_var)
        B[6].rtoks.append(tok)
        k.op("vector", lambda v: v.tensor_scalar(
            out=st_ln.ap, in0=st_var.ap, scalar1=1e-5, scalar2=None,
            op0=ALU.add), ins=[st_var], out=st_ln)
        k.op("scalar", lambda s: s.activation(out=st_var.ap, in_=st_ln.ap,
                                              func=ACTF.Ln),
             ins=[st_ln], out=st_var)
        k.op("scalar", lambda s: s.activation(out=st_rs.ap, in_=st_var.ap,
                                              func=ACTF.Exp, scale=-0.5),
             ins=[st_var], out=st_rs)
        k.op("vector", lambda v: v.tensor_scalar(
            out=st_nmu.ap, in0=st_mu.ap, scalar1=-1.0, scalar2=None,
            op0=ALU.mult), ins=[st_mu], out=st_nmu)
        k.mm(B[5], B[5].ap[:, :], ones1x128b.ap, st_nmu.ap, True, True,
             ins=[ones1x128b, st_nmu])
        k.mm(B[6], B[6].ap[:, :], ones1x128f.ap, st_rs.ap, True, True,
             ins=[ones1x128f, st_rs])
        for dc in range(NDC):
            sl = dc % 2
            xd = x_sb[:, C * dc:C * dc + C]
            t_a = k.op("vector", lambda v, xd=xd, sl=sl: v.tensor_tensor(
                out=t1_buf[sl].ap, in0=xd, in1=B[5].ap, op=ALU.add),
                ins=[x_sb, B[5]], out=t1_buf[sl])
            B[5].rtoks.append(t_a)
            t_b = k.op("vector", lambda v, sl=sl: v.tensor_tensor(
                out=xnf_buf[sl].ap, in0=t1_buf[sl].ap, in1=B[6].ap,
                op=ALU.mult), ins=[t1_buf[sl], B[6]], out=xnf_buf[sl])
            B[6].rtoks.append(t_b)
            k.op("scalar", lambda s, dc=dc, sl=sl: s.copy(
                out=xnv[dc].ap, in_=xnf_buf[sl].ap),
                ins=[xnf_buf[sl]], out=xnv[dc])
            k.dma("sync", "si_sc",
                  lambda e, dc=dc, cs=cs, sl=sl: e.dma_start(
                      out=xn_scr[128 * dc:128 * dc + 128, cs],
                      in_=xnf_buf[sl].ap), ins=[xnf_buf[sl]])

        def qk_mms(oc):
            bq = B[oc % 2]
            bk = B[2 + oc % 2]
            for dc in range(NDC):
                k.mm(bq, bq.ap[:, :],
                     wqkv_sb[:, 3 * D * dc + 128 * oc:
                             3 * D * dc + 128 * oc + 128],
                     xnv[dc].ap, dc == 0, False,
                     ins=[wqkv_sb, xnv[dc]])
            k.mm(bq, bq.ap[:, :], bqkv_row[0:1, 128 * oc:128 * oc + 128],
                 onesrowb.ap, False, True, ins=[bqkv_row, onesrowb])
            for dc in range(NDC):
                k.mm(bk, bk.ap[:, :],
                     wqkv_sb[:, 3 * D * dc + D + 128 * oc:
                             3 * D * dc + D + 128 * oc + 128],
                     xnv[dc].ap, dc == 0, False,
                     ins=[wqkv_sb, xnv[dc]])
            k.mm(bk, bk.ap[:, :], bqkv_row[0:1, D + 128 * oc:D + 128 * oc + 128],
                 onesrowb.ap, False, True, ins=[bqkv_row, onesrowb])
            return bq, bk

        prev = None
        for oc in range(NDC):
            bq, bk = qk_mms(oc)
            if prev is not None:
                features(prev[0], prev[1], qfv[prev[0]])
                features(prev[0], prev[2], kfv[prev[0]])
            prev = (oc, bq, bk)
        features(prev[0], prev[1], qfv[prev[0]])
        features(prev[0], prev[2], kfv[prev[0]])

        for tt in range(4):
            for half in range(2):
                bank = B[5 + half]
                for dc in range(NDC):
                    k.mm(bank, bank.ap[:, :],
                         xnv[dc].ap[:, 128 * tt:128 * tt + 128],
                         wqkv_sb[:, 3 * D * dc + 2 * D + C * half:
                                 3 * D * dc + 2 * D + C * half + C],
                         dc == 0, False, ins=[xnv[dc], wqkv_sb])
                k.mm(bank, bank.ap[:, :], ones1x128b.ap,
                     bv_row_sb[0:1, C * half:C * half + C], False, True,
                     ins=[ones1x128b, bv_row_sb])
                src = bank.ap.rearrange("p (h w) -> p h w", w=64)
                dst = vv[tt].ap[:, 520 * half:520 * half + 520]
                dst = dst.rearrange("p (h w) -> p h w", w=65)[:, :, 0:64]
                tok = k.op("vector", lambda v, d=dst, s=src: v.tensor_copy(
                    out=d, in_=s), ins=[bank], out=vv[tt])
                bank.rtoks.append(tok)

        for i in range(NDC):
            k.op("vector", lambda v, i=i, Scur=Scur: v.tensor_copy(
                out=SZ_bf.ap, in_=Scur.ap[:, 65 * i:65 * i + 65]),
                ins=[Scur], out=SZ_bf)
            k.op("vector", lambda v, i=i, Scur=Scur: v.tensor_copy(
                out=zq_sb[:, 0:1], in_=Scur.ap[:, 65 * i + 64:65 * i + 65]),
                ins=[Scur], out=zq_sb)
            k.op("vector", lambda v, i=i, Scur=Scur: v.tensor_copy(
                out=zq_sb[:, 3:4], in_=Scur.ap[:, 65 * i + 64:65 * i + 65]),
                ins=[Scur], out=zq_sb)
            for j in range(4):
                k.mm(PT, PT.ap[:, 128 * j:128 * j + 128],
                     kfv[i].ap[:, 128 * j:128 * j + 128],
                     ident_sb.ap, j == 0, j == 3,
                     ins=[kfv[i], ident_sb], is_transpose=True)
            tok = k.op("vector", lambda v: v.tensor_copy(
                out=kfT_sb.ap, in_=PT.ap[:, 0:512]), ins=[PT], out=kfT_sb)
            PT.rtoks.append(tok)
            for par in range(2):
                qr = slice(64 * par, 64 * par + 64)
                for j in range(4):
                    bank = B[par + 2 * (j % 2)]
                    k.mm(bank, bank.ap[:, :],
                         kfv[i].ap[qr, 128 * j:128 * j + 128],
                         qfv[i].ap[qr, :], True, True,
                         ins=[kfv[i], qfv[i]])
                    base = 1280 * par + ATM_OFF[j]
                    t1 = k.op("vector", lambda v, bank=bank, base=base, j=j:
                              v.tensor_tensor(
                                  out=atm_sb[:, base:base + 128],
                                  in0=bank.ap[:, 128 * j:128 * j + 128],
                                  in1=dmask_sb.ap, op=ALU.mult),
                              ins=[bank, dmask_sb], out=atm_sb)
                    bank.rtoks.append(t1)
                    if j < 3:
                        w = ATM_W[j] - 128
                        t2 = k.op("scalar", lambda s, bank=bank, base=base,
                                  j=j, w=w: s.copy(
                                      out=atm_sb[:, base + 128:base + 128 + w],
                                      in_=bank.ap[:, 128 * (j + 1):C]),
                                  ins=[bank], out=atm_sb)
                        bank.rtoks.append(t2)
            for par in range(2):
                h = 2 * i + par
                qr = slice(64 * par, 64 * par + 64)
                k.mm(B[4], B[4].ap[qr, :],
                     SZ_bf.ap[qr, 0:64], qfv[i].ap[qr, :],
                     True, False, ins=[SZ_bf, qfv[i]])
                for j in range(4):
                    k.mm(B[4], B[4].ap[qr, 128 * j:C],
                         vv[j].ap[:, 65 * h:65 * h + 64],
                         atm_sb.ap[:, 1280 * par + ATM_OFF[j]:
                                   1280 * par + ATM_OFF[j] + ATM_W[j]],
                         False, par == 1 and j == 3, ins=[vv[j], atm_sb])
            for par in range(2):
                qr = slice(64 * par, 64 * par + 64)
                zsl = (slice(0, 64), slice(0, 2)) if par == 0 else \
                    (slice(64, 128), slice(2, 4))
                k.mm(B[5], B[5].ap[0:2, :],
                     zq_sb.ap[zsl[0], zsl[1]], qfv[i].ap[qr, :],
                     par == 0, False, ins=[zq_sb, qfv[i]])
                for j in range(4):
                    k.mm(B[5], B[5].ap[0:2, 128 * j:C],
                         onescol3.ap[:, par:par + 2],
                         atm_sb.ap[:, 1280 * par + ATM_OFF[j]:
                                   1280 * par + ATM_OFF[j] + ATM_W[j]],
                         False, par == 1 and j == 3, ins=[onescol3, atm_sb])
            for par in range(2):
                h = 2 * i + par
                for j in range(4):
                    k.mm(B[6], B[6].ap[64 * par:64 * par + 64, 0:65],
                         kfT_sb.ap[:, 128 * j + 64 * par:
                                   128 * j + 64 * par + 64],
                         vv[j].ap[:, 65 * h:65 * h + 65],
                         j == 0, par == 1 and j == 3, ins=[kfT_sb, vv[j]])
            tok = k.op("vector", lambda v: v.tensor_scalar(
                out=den_sb.ap, in0=B[5].ap[0:2, :], scalar1=1e-6,
                scalar2=None, op0=ALU.max), ins=[B[5]], out=den_sb)
            B[5].rtoks.append(tok)
            k.op("scalar", lambda s: s.activation(out=dln_sb.ap,
                                                  in_=den_sb.ap, func=ACTF.Ln),
                 ins=[den_sb], out=dln_sb)
            k.op("scalar", lambda s: s.activation(out=recip_sb.ap,
                                                  in_=dln_sb.ap,
                                                  func=ACTF.Exp, scale=-1.0),
                 ins=[dln_sb], out=recip_sb)
            k.mm(B[5], B[5].ap[:, :], selpair_sb.ap, recip_sb.ap, True, True,
                 ins=[selpair_sb, recip_sb])
            tok = k.op("vector", lambda v, i=i, Scur=Scur, Snew=Snew:
                       v.tensor_tensor(
                           out=Snew.ap[:, 65 * i:65 * i + 65],
                           in0=B[6].ap[:, 0:65],
                           in1=Scur.ap[:, 65 * i:65 * i + 65], op=ALU.add),
                       ins=[B[6], Scur], out=Snew)
            B[6].rtoks.append(tok)
            tok = k.op("vector", lambda v: v.tensor_copy(
                out=ctx_sb.ap, in_=B[4].ap), ins=[B[4]], out=ctx_sb)
            B[4].rtoks.append(tok)
            tok = k.op("vector", lambda v, i=i: v.tensor_tensor(
                out=attnv[i].ap, in0=ctx_sb.ap, in1=B[5].ap,
                op=ALU.mult), ins=[ctx_sb, B[5]], out=attnv[i])
            B[5].rtoks.append(tok)

        for oc in range(NDC):
            bank = B[oc % 2]
            for dc in range(NDC):
                k.mm(bank, bank.ap[:, :],
                     wout_sb[:, D * dc + 128 * oc:D * dc + 128 * oc + 128],
                     attnv[dc].ap, dc == 0, dc == NDC - 1,
                     ins=[wout_sb, attnv[dc]])
            sl = oc % 2
            tok = k.op("scalar", lambda s, bank=bank, oc=oc, sl=sl:
                       s.activation(out=sout_buf[sl].ap, in_=bank.ap,
                                    func=ACTF.Identity,
                                    bias=bout_sb.ap[:, oc:oc + 1]),
                       ins=[bank, bout_sb], out=sout_buf[sl])
            bank.rtoks.append(tok)
            k.dma("sync", "si_sc",
                  lambda e, oc=oc, cs=cs, sl=sl: e.dma_start(
                      out=s_scr[128 * oc:128 * oc + 128, cs],
                      in_=sout_buf[sl].ap), ins=[sout_buf[sl]])

    pass1_toks = [(s, k.cnt[s]) for s in ("e_t", "e_v", "e_s", "e_g",
                                          "si_x", "si_sc", "sg", "sg_su",
                                          "si_m")]
    barrier_toks = []
    for ename in ("tensor", "vector", "scalar", "gpsimd", "sync"):
        tok = k._emit(ename, lambda e: e.nop(), pass1_toks, ESEM[ename], 1)
        barrier_toks.append(tok)
    for bk in B + [PT]:
        bk.rtoks = list(barrier_toks)
        bk.wtok = None
    es1.close()

    es2 = contextlib.ExitStack()
    h_sb = [k.sb(f"h_sb{p}", [128, NDC * C], F32, es=es2) for p in range(2)]
    hv = [K.views(h_sb[p], NDC, C) for p in range(2)]
    h_bf = k.sb("h_bf", [128, NDC * C], BF16, es=es2)
    hbv = K.views(h_bf, NDC, C)
    g1_bf = k.sb("g1_bf", [128, NDC * C], BF16, es=es2)
    g1v = K.views(g1_bf, NDC, C)
    y_sb = k.sb("y_sb", [128, NDC * C], F32, es=es2)
    yv = K.views(y_sb, NDC, C)
    wfull = [k.sb(f"wfull{p}", [128, NDC * D], BF16, es=es2) for p in range(2)]
    u_bf = k.sb("u_bf", [64, C], BF16, es=es2)
    mub_sb = k.sb("mub_sb", [1, C], BF16, es=es2)
    sqy_buf = [k.sb("sqy0", [128, C], BF16, es=es2)] * 2
    t4_buf = [k.sb("t4_0", [128, C], F32, es=es2)] * 2
    t5_buf = [k.sb("t5_0", [128, C], F32, es=es2)] * 2
    sg_buf = [k.sb("sgb0", [128, C], F32, es=es2)] * 2
    gg_buf = [k.sb("ggb0", [128, C], F32, es=es2)] * 2
    e2_buf = sg_buf
    s_in = [k.sb(f"s_in{i}", [128, C], F32, es=es2) for i in range(2)]
    x_in = [k.sb(f"x_in{i}", [128, C], F32, es=es2) for i in range(2)]
    o_buf = [k.sb(f"o_buf{i}", [128, C], F32, es=es2) for i in range(2)]
    wbar_t = k.sb("wbar_t", [128, 1], F32, es=es2)

    def load_w(l, slot, extra=()):
        for dc in range(NDC):
            r = slice(D * l + 128 * dc, D * l + 128 * dc + 128)
            k.dma("sync", "si_w",
                  lambda e, r=r, dc=dc, slot=slot: e.dma_start(
                      out=wfull[slot][:, D * dc:D * dc + D], in_=w_scr[r, :]),
                  ins=list(extra), out=wfull[slot], waw=False)

    sg_done = ("sg", k.cnt["sg"])
    load_w(0, 0, extra=[sg_done])
    load_w(1, 1, extra=[sg_done])

    for c in range(NCH):
        cs = slice(C * c, C * c + C)
        for dc in range(NDC):
            k.dma("sync", "si_x",
                  lambda e, dc=dc, cs=cs: e.dma_start(
                      out=hv[0][dc].ap,
                      in_=xn_scr[128 * dc:128 * dc + 128, cs]),
                  out=hv[0][dc])
        for dc in range(NDC):
            k.op("scalar", lambda s, dc=dc: s.copy(
                out=hbv[dc].ap, in_=hv[0][dc].ap),
                ins=[hv[0][dc]], out=hbv[dc])
        for blk in range(3):
            lA, lB = 2 * blk, 2 * blk + 1
            wA, wB = lA % 2, lB % 2
            hcurv, hnewv = hv[blk % 2], hv[1 - blk % 2]
            ur = slice(0, 32)
            for dc in range(NDC):
                k.mm(B[4], B[4].ap[ur, :],
                     laT_sb[:, 192 * dc + 32 * lA:192 * dc + 32 * lA + 32],
                     hbv[dc].ap, dc == 0, dc == NDC - 1,
                     ins=[laT_sb, hbv[dc]])
            tok = k.op("vector", lambda v, ur=ur: v.tensor_copy(
                out=u_bf.ap[ur, :], in_=B[4].ap[ur, :]), ins=[B[4]], out=u_bf)
            B[4].rtoks.append(tok)
            if c == 0:
                for dc in range(NDC):
                    k.op("vector", lambda v, dc=dc, wA=wA: v.tensor_reduce(
                        out=wbar_t.ap, in_=wfull[wA][:, D * dc:D * dc + D],
                        axis=AXX, op=ALU.add), ins=[wfull[wA]], out=wbar_t)
                    k.op("vector", lambda v, dc=dc, lA=lA: v.tensor_copy(
                        out=wbar_sb[:, 8 * lA + dc:8 * lA + dc + 1],
                        in_=wbar_t.ap), ins=[wbar_t], out=wbar_sb)
            k.mm(B[5], B[5].ap[0:1, :], lbar_sb.ap[ur, blk:blk + 1],
                 u_bf.ap[ur, :], True, False, ins=[lbar_sb, u_bf])
            for dc in range(NDC):
                k.mm(B[5], B[5].ap[0:1, :],
                     wbar_sb[:, 8 * lA + dc:8 * lA + dc + 1],
                     hbv[dc].ap, False, False,
                     ins=[wbar_sb, hbv[dc]])
            k.mm(B[5], B[5].ap[0:1, :], bbar_sb.ap[0:1, lA:lA + 1],
                 onesrowb.ap, False, True, ins=[bbar_sb, onesrowb])
            tok = k.op("vector", lambda v: v.tensor_scalar(
                out=mub_sb.ap, in0=B[5].ap[0:1, :], scalar1=1.0 / 1024.0,
                scalar2=None, op0=ALU.mult), ins=[B[5]], out=mub_sb)
            B[5].rtoks.append(tok)
            tap("u0", u_bf, c == 0 and blk == 0)
            tap("mu0", mub_sb, c == 0 and blk == 0)
            for oc in range(NDC):
                bank = B[oc % 2]
                k.mm(bank, bank.ap[:, :],
                     lbT_sb.ap[ur, D * blk + 128 * oc:D * blk + 128 * oc + 128],
                     u_bf.ap[ur, :], True, False, ins=[lbT_sb, u_bf])
                for dc in range(NDC):
                    k.mm(bank, bank.ap[:, :],
                         wfull[wA][:, D * dc + 128 * oc:D * dc + 128 * oc + 128],
                         hbv[dc].ap, False, False,
                         ins=[wfull[wA], hbv[dc]])
                k.mm(bank, bank.ap[:, :], nones1x128b.ap, mub_sb.ap,
                     False, True, ins=[nones1x128b, mub_sb])
                sl = oc % 2
                tok = k.op("scalar", lambda s, bank=bank, oc=oc, lA=lA:
                           s.activation(out=yv[oc].ap,
                                        in_=bank.ap, func=ACTF.Identity,
                                        bias=bcol_sb.ap[:, 8 * lA + oc:
                                                        8 * lA + oc + 1]),
                           ins=[bank, bcol_sb], out=yv[oc])
                bank.rtoks.append(tok)
                k.op("vector", lambda v, oc=oc, sl=sl: v.tensor_tensor(
                    out=sqy_buf[sl].ap, in0=yv[oc].ap,
                    in1=yv[oc].ap, op=ALU.mult),
                    ins=[yv[oc]], out=sqy_buf[sl])
                k.mm(B[5], B[5].ap[0:1, :], onesc128b.ap, sqy_buf[sl].ap,
                     oc == 0, oc == NDC - 1, ins=[onesc128b, sqy_buf[sl]])
            nl = 2 * blk + 2
            if nl < 6:
                load_w(nl, nl % 2)
            elif c + 1 < NCH:
                load_w(0, 0)
            tap("y0", y_sb, c == 0 and blk == 0)
            tap("y1", y_sb, c == 0 and blk == 1)
            tok = k.op("vector", lambda v: v.tensor_scalar(
                out=st_var.ap, in0=B[5].ap[0:1, :], scalar1=1.0 / 1024.0,
                scalar2=None, op0=ALU.mult), ins=[B[5]], out=st_var)
            B[5].rtoks.append(tok)
            k.op("vector", lambda v: v.tensor_scalar(
                out=st_ln.ap, in0=st_var.ap, scalar1=1e-5, scalar2=None,
                op0=ALU.add), ins=[st_var], out=st_ln)
            k.op("scalar", lambda s: s.activation(out=st_var.ap, in_=st_ln.ap,
                                                  func=ACTF.Ln),
                 ins=[st_ln], out=st_var)
            k.op("scalar", lambda s: s.activation(out=st_rs.ap, in_=st_var.ap,
                                                  func=ACTF.Exp, scale=-0.5),
                 ins=[st_var], out=st_rs)
            k.mm(B[6], B[6].ap[:, :], ones1x128f.ap, st_rs.ap, True, True,
                 ins=[ones1x128f, st_rs])
            for oc in range(NDC):
                sl = oc % 2
                tok = k.op("vector", lambda v, oc=oc, sl=sl, blk=blk:
                           v.scalar_tensor_tensor(
                               out=t4_buf[sl].ap,
                               in0=yv[oc].ap,
                               scalar=gam_sb.ap[:, 8 * blk + oc:
                                                8 * blk + oc + 1],
                               in1=B[6].ap, op0=ALU.mult, op1=ALU.mult),
                           ins=[yv[oc], gam_sb, B[6]], out=t4_buf[sl])
                B[6].rtoks.append(tok)
                k.op("vector", lambda v, sl=sl, blk=blk, oc=oc: v.tensor_scalar(
                    out=t5_buf[sl].ap, in0=t4_buf[sl].ap,
                    scalar1=bet_sb.ap[:, 8 * blk + oc:8 * blk + oc + 1],
                    scalar2=None, op0=ALU.add),
                    ins=[t4_buf[sl], bet_sb], out=t5_buf[sl])
                k.op("scalar", lambda s, oc=oc, sl=sl: s.activation(
                    out=g1v[oc].ap, in_=t5_buf[sl].ap,
                    func=ACTF.Gelu), ins=[t5_buf[sl]], out=g1v[oc])
            tap("g10", g1_bf, c == 0 and blk == 0)
            tap("g11", g1_bf, c == 0 and blk == 1)
            tap("rs0", st_rs, c == 0 and blk == 0)
            ur2 = slice(32, 64)
            for dc in range(NDC):
                k.mm(B[4], B[4].ap[ur2, :],
                     laT_sb[:, 192 * dc + 32 * lB:192 * dc + 32 * lB + 32],
                     g1v[dc].ap, dc == 0, dc == NDC - 1,
                     ins=[laT_sb, g1v[dc]])
            tok = k.op("vector", lambda v, ur2=ur2: v.tensor_copy(
                out=u_bf.ap[ur2, :], in_=B[4].ap[ur2, :]), ins=[B[4]],
                out=u_bf)
            B[4].rtoks.append(tok)
            for oc in range(NDC):
                bank = B[2 + oc % 2]
                k.mm(bank, bank.ap[:, :],
                     lbT_sb.ap[ur2, D * blk + 128 * oc:D * blk + 128 * oc + 128],
                     u_bf.ap[ur2, :], True, False, ins=[lbT_sb, u_bf])
                for dc in range(NDC):
                    k.mm(bank, bank.ap[:, :],
                         wfull[wB][:, D * dc + 128 * oc:D * dc + 128 * oc + 128],
                         g1v[dc].ap, False, dc == NDC - 1,
                         ins=[wfull[wB], g1v[dc]])
                tok = k.op("vector", lambda v, bank=bank, oc=oc, lB=lB,
                           hc=hcurv[oc], hn=hnewv[oc]: v.scalar_tensor_tensor(
                               out=hn.ap, in0=bank.ap,
                               scalar=bcol_sb.ap[:, 8 * lB + oc:
                                                 8 * lB + oc + 1],
                               in1=hc.ap,
                               op0=ALU.add, op1=ALU.add),
                           ins=[bank, bcol_sb, hcurv[oc]], out=hnewv[oc])
                bank.rtoks.append(tok)
                k.op("scalar", lambda s, oc=oc, hn=hnewv[oc]: s.copy(
                    out=hbv[oc].ap, in_=hn.ap), ins=[hnewv[oc]], out=hbv[oc])
            tap("h1", h_sb[1 - blk % 2], c == 0 and blk == 0)
            nl2 = 2 * blk + 3
            if nl2 < 6:
                load_w(nl2, 1)
            elif c + 1 < NCH:
                load_w(1, 1)
        hfinv = hv[1]
        for oc in range(NDC):
            sl = oc % 2
            k.dma("sync", "si_x",
                  lambda e, oc=oc, cs=cs, sl=sl: e.dma_start(
                      out=s_in[sl].ap,
                      in_=s_scr[128 * oc:128 * oc + 128, cs]), out=s_in[sl])
            k.dma("sync", "si_x",
                  lambda e, oc=oc, cs=cs, sl=sl: e.dma_start(
                      out=x_in[sl].ap,
                      in_=ap["xT"][128 * oc:128 * oc + 128, cs]), out=x_in[sl])
            k.op("scalar", lambda s, sl=sl, hf=hfinv[oc]: s.activation(
                out=sg_buf[sl].ap, in_=hf.ap,
                func=ACTF.Sigmoid), ins=[hfinv[oc]], out=sg_buf[sl])
            k.op("vector", lambda v, sl=sl, hf=hfinv[oc]: v.tensor_tensor(
                out=gg_buf[sl].ap, in0=hf.ap,
                in1=sg_buf[sl].ap, op=ALU.mult),
                ins=[hfinv[oc], sg_buf[sl]], out=gg_buf[sl])
            k.op("vector", lambda v, sl=sl: v.tensor_tensor(
                out=e2_buf[sl].ap, in0=gg_buf[sl].ap, in1=s_in[sl].ap,
                op=ALU.mult), ins=[gg_buf[sl], s_in[sl]], out=e2_buf[sl])
            k.op("vector", lambda v, sl=sl: v.tensor_tensor(
                out=o_buf[sl].ap, in0=e2_buf[sl].ap, in1=x_in[sl].ap,
                op=ALU.add), ins=[e2_buf[sl], x_in[sl]], out=o_buf[sl])
            k.dma("sync", "si_out",
                  lambda e, oc=oc, cs=cs, sl=sl: e.dma_start(
                      out=outT[128 * oc:128 * oc + 128, cs],
                      in_=o_buf[sl].ap), ins=[o_buf[sl]])

    es2.close()
    final = [("si_out", k.cnt["si_out"])]
    k.finalize(final)
    k.es.close()
    return nc


def _host_constants():
    c = {}
    c["ident"] = np.eye(128, dtype=BF)
    p = np.arange(128)
    c["diagmask"] = (p[:, None] <= p[None, :]).astype(BF)
    c["selhalf"] = (p[:, None] // 64 == np.arange(2)[None, :]).astype(BF)
    c["selpair_f"] = (np.arange(2)[:, None] == p[None, :] // 64).astype(np.float32)
    c["selrq_f"] = (np.arange(2)[:, None] == (p[None, :] % 64) // 32).astype(np.float32)
    c["ones_1x128_b"] = np.ones((1, 128), BF)
    c["nones_1x128_b"] = -np.ones((1, 128), BF)
    c["ones_1x128_f"] = np.ones((1, 128), np.float32)
    c["ones_c128_b"] = np.ones((128, 1), BF)
    c["ones_c128_f"] = np.ones((128, 1), np.float32)
    c["ones_row_b"] = np.ones((1, C), BF)
    oc3 = np.zeros((128, 3), BF)
    oc3[:, 0] = 1.0
    oc3[:, 2] = 1.0
    c["onescol3"] = oc3
    return c


def _fm_cols(v):
    n = v.shape[0]
    out = np.zeros((128, 8 * n), np.float32)
    for i in range(n):
        out[:, 8 * i:8 * i + 8] = v[i].reshape(8, 128).T
    return out


def prep_in_maps(L0_attn_x, L0_attn_Wqkv, L0_attn_bqkv, L0_attn_Wout,
                 L0_attn_bout, L0_attn_omega, L0_attn_quad_nodes,
                 L0_attn_quad_weights, L0_ffn_qweights, L0_ffn_scales,
                 L0_ffn_biases, L0_ffn_lora_a, L0_ffn_lora_b,
                 L0_ffn_ln_gamma, L0_ffn_ln_beta):
    x = np.ascontiguousarray(np.asarray(L0_attn_x, np.float32))
    Wqkv = np.ascontiguousarray(np.asarray(L0_attn_Wqkv, np.float32))
    bqkv = np.asarray(L0_attn_bqkv, np.float32)
    Wout = np.ascontiguousarray(np.asarray(L0_attn_Wout, np.float32))
    bout = np.asarray(L0_attn_bout, np.float32)
    omega = np.asarray(L0_attn_omega, np.float32)
    qnodes = np.asarray(L0_attn_quad_nodes, np.float32)
    qweights = np.asarray(L0_attn_quad_weights, np.float32)
    qint = np.ascontiguousarray(np.asarray(L0_ffn_qweights, np.int32))
    scales = np.asarray(L0_ffn_scales, np.float32)
    biases = np.asarray(L0_ffn_biases, np.float32)
    lora_a = np.asarray(L0_ffn_lora_a, np.float32)
    lora_b = np.asarray(L0_ffn_lora_b, np.float32)
    ln_g = np.asarray(L0_ffn_ln_gamma, np.float32)
    ln_b = np.asarray(L0_ffn_ln_beta, np.float32)

    omgblk = np.zeros((128, 8 * 128), np.float32)
    for i in range(8):
        for par in range(2):
            h = 2 * i + par
            O = np.transpose(omega[:, h], (1, 0, 2)).reshape(64, 64)
            omgblk[64 * par:64 * par + 64,
                   128 * i + 64 * par:128 * i + 64 * par + 64] = O
    qT = np.ascontiguousarray(qint.transpose(0, 2, 1)).reshape(6 * D, D)
    scT = np.ascontiguousarray(
        np.repeat(scales, 16, axis=2).transpose(0, 2, 1)).reshape(6 * D, D)
    laT = np.ascontiguousarray(lora_a.transpose(2, 0, 1)).reshape(D, 192)
    lbT = np.zeros((64, 3 * D), np.float32)
    for b_ in range(3):
        lbT[0:32, D * b_:D * (b_ + 1)] = lora_b[2 * b_].T
        lbT[32:64, D * b_:D * (b_ + 1)] = lora_b[2 * b_ + 1].T

    shared = {
        "wqkv": Wqkv, "wout": Wout,
        "bq_row": np.ascontiguousarray(bqkv[None, 0:D]),
        "bk_row": np.ascontiguousarray(bqkv[None, D:2 * D]),
        "bv_row": np.ascontiguousarray(bqkv[None, 2 * D:3 * D]),
        "bout_fm": np.ascontiguousarray(bout.reshape(8, 128).T),
        "omgblk": omgblk,
        "qn2": np.ascontiguousarray(qnodes.reshape(2, 1)),
        "qw2": np.ascontiguousarray(qweights.reshape(2, 1)),
        "qT": qT, "scT": scT,
        "bcol_fm": _fm_cols(biases), "gam_fm": _fm_cols(ln_g),
        "bet_fm": _fm_cols(ln_b),
        "laT": laT, "lbT": lbT,
    }
    shared.update(_host_constants())

    in_maps = []
    for b_ in range(8):
        m = dict(shared)
        m["xT"] = np.ascontiguousarray(x[b_].T)
        in_maps.append(m)
    return in_maps


def kernel(**inputs):
    in_maps = prep_in_maps(**inputs)
    nc = build_program()
    res = run_bass_kernel_spmd(nc, in_maps, core_ids=list(range(8)))
    out = np.stack([np.ascontiguousarray(np.asarray(r["outT"]).T)
                    for r in res.results])
    return np.ascontiguousarray(out)


# version 09
# speedup vs baseline: 1.0224x; 1.0224x over previous
import contextlib
import numpy as np
import ml_dtypes

import concourse.bass as bass
import concourse.mybir as mybir
from concourse.bass_utils import run_bass_kernel_spmd

F32 = mybir.dt.float32
BF16 = mybir.dt.bfloat16
I32 = mybir.dt.int32
ALU = mybir.AluOpType
ACTF = mybir.ActivationFunctionType
BF = ml_dtypes.bfloat16
AXX = mybir.AxisListType.X

D = 1024
T = 4096
C = 512
NCH = T // C
NDC = D // 128
ATM_OFF = (0, 512, 896, 1152)
ATM_W = (512, 384, 256, 128)

ENGINES = ("sync", "gpsimd", "tensor", "vector", "scalar")
ESEM = {"tensor": "e_t", "vector": "e_v", "scalar": "e_s", "gpsimd": "e_g",
        "sync": "e_y"}


class Buf:
    __slots__ = ("ap", "wtok", "rtoks")

    def __init__(self, ap):
        self.ap = ap
        self.wtok = None
        self.rtoks = []

    def __getitem__(self, key):
        return self.ap[key]


class Bank(Buf):
    __slots__ = ("name",)

    def __init__(self, ap, name):
        super().__init__(ap)
        self.name = name


class K:
    def __init__(self, nc):
        self.nc = nc
        self.es = contextlib.ExitStack()
        self.ops = {e: [] for e in ENGINES}
        self.cnt = {}
        for s in ("si_x", "si_w", "si_m", "si_sc", "si_out", "sg", "sg_su",
                  "sg_tap", "e_t", "e_v", "e_s", "e_g", "e_y"):
            self.cnt[s] = 0
        self._win = []

    def sb(self, name, shape, dtype, es=None):
        h = (es or self.es).enter_context(
            self.nc.sbuf_tensor(name, list(shape), dtype))
        return Buf(h[:])

    @staticmethod
    def views(buf, n, w):
        return [Buf(buf.ap[:, w * i:w * (i + 1)]) for i in range(n)]

    def bank(self, name, dtype=F32, cols=None):
        if cols is None:
            cols = 512 if dtype == F32 else 1024
        h = self.es.enter_context(self.nc.psum_tensor(name, [128, cols], dtype))
        return Bank(h[:], name)

    paranoid = 0

    def _emit(self, eng, fn, waits, inc, d):
        if self.paranoid == 1:
            waits = list(waits) + [(s, v) for s, v in self.cnt.items() if v > 0]
        elif self.paranoid == 2:
            waits = list(waits) + [(s, v) for s, v in self.cnt.items()
                                   if v > 0 and s.startswith("e_")]
        elif self.paranoid == 3:
            waits = list(waits) + [(s, v) for s, v in self.cnt.items()
                                   if v > 0 and not s.startswith("e_")]
        self.cnt[inc] += d
        self.ops[eng].append((fn, tuple(waits), (inc, d)))
        return (inc, self.cnt[inc])

    @staticmethod
    def _waits(ins):
        ws = []
        for o in ins:
            if o is None:
                continue
            if isinstance(o, tuple):
                ws.append(o)
            elif o.wtok is not None:
                ws.append(o.wtok)
        return ws

    def _do(self, eng, fn, ins, out, sem, d, waw=True):
        waits = self._waits(ins)
        if out is not None:
            if waw and out.wtok is not None:
                waits.append(out.wtok)
            waits += out.rtoks
        waits = [(s, self.cnt[s]) if not s.startswith("e_") else (s, v)
                 for (s, v) in waits]
        tok = self._emit(eng, fn, waits, sem, d)
        for o in ins:
            if o is not None and not isinstance(o, tuple):
                o.rtoks.append(tok)
        if out is not None:
            out.wtok = tok
            out.rtoks = []
        return tok

    def op(self, eng, fn, ins=(), out=None):
        return self._do(eng, fn, ins, out, ESEM[eng], 1)

    def dma(self, eng, sem, fn, ins=(), out=None, waw=True):
        return self._do(eng, fn, ins, out, sem, 16, waw=waw)

    def mm(self, bank, out_ap, lhsT, rhs, start, stop, ins=(),
           is_transpose=False):
        base = lhsT.base_partition()
        lo, hi = base // 32, (base + lhsT.shape[0] + 31) // 32
        if hi - lo >= 4:
            self._win = []
        else:
            for (bn, l2, h2) in self._win:
                if bn == bank.name and (h2 <= lo or hi <= l2):
                    raise RuntimeError(
                        f"strip conflict bank={bank.name} ({l2},{h2})vs({lo},{hi})")
            self._win = [w for w in self._win if w[0] != bank.name] + \
                [(bank.name, lo, hi)]
        ins = list(ins)
        if start:
            ins += bank.rtoks
            bank.rtoks = []
        kw = {"is_transpose": True} if is_transpose else {}
        tok = self.op("tensor",
                      lambda t, o=out_ap, a=lhsT, b=rhs, s1=start, s2=stop, kw=kw:
                      t.matmul(o, a, b, start=s1, stop=s2, **kw),
                      ins=ins)
        bank.wtok = tok
        return tok

    def finalize(self, final_waits):
        nc = self.nc
        referenced = {}
        for ename in ENGINES:
            for _, waits, _ in self.ops[ename]:
                for (s, v) in waits:
                    referenced.setdefault(s, set()).add(v)
        for (s, v) in final_waits:
            referenced.setdefault(s, set()).add(v)
        remap = {}
        newcnt = {}
        for s in self.cnt:
            if not s.startswith("e_"):
                continue
            keep = sorted(referenced.get(s, ()))
            remap[s] = {v: i + 1 for i, v in enumerate(keep)}
            newcnt[s] = len(keep)

        def map_wait(s, v):
            if s in remap:
                return (s, remap[s][v])
            return (s, v)

        sems = {}
        with contextlib.ExitStack() as st:
            for name in self.cnt:
                sems[name] = st.enter_context(nc.semaphore(name))
            block = st.enter_context(nc.Block())

            def mk(ename):
                def body(eng):
                    hw = {}
                    run = {s: 0 for s in self.cnt}
                    for fn, waits, (isem, d) in self.ops[ename]:
                        for (s0, v0) in waits:
                            s, v = map_wait(s0, v0)
                            if hw.get(s, 0) < v:
                                eng.wait_ge(sems[s], v)
                                hw[s] = v
                        inst = fn(eng)
                        if isem in remap:
                            run[isem] += d
                            nv = remap[isem].get(run[isem])
                            if nv is not None:
                                inst.then_inc(sems[isem], nv - remap[isem].get(
                                    max((x for x in remap[isem] if x < run[isem]),
                                        default=0), 0) if False else 1)
                        else:
                            inst.then_inc(sems[isem], d)
                    if ename == "sync":
                        for (s0, v0) in final_waits:
                            s, v = map_wait(s0, v0)
                            eng.wait_ge(sems[s], v)
                return body

            for ename in ENGINES:
                getattr(block, ename)(mk(ename))


def build_program(taps=(), paranoid=0):
    nc = bass.Bass("TRN2", target_bir_lowering=False)
    k = K(nc)
    k.paranoid = paranoid
    ap = {}

    def inp(name, shape, dtype):
        ap[name] = nc.dram_tensor(name, list(shape), dtype,
                                  kind="ExternalInput").ap()

    inp("xT", [D, T], F32)
    inp("wqkv", [D, 3 * D], F32)
    inp("wout", [D, D], F32)
    inp("bq_row", [1, D], F32)
    inp("bk_row", [1, D], F32)
    inp("bv_row", [1, D], F32)
    inp("bout_fm", [128, NDC], F32)
    inp("omgblk", [128, 8 * 128], F32)
    inp("qn2", [2, 1], F32)
    inp("qw2", [2, 1], F32)
    inp("qT", [6 * D, D], I32)
    inp("scT", [6 * D, D], F32)
    inp("bcol_fm", [128, 48], F32)
    inp("gam_fm", [128, 24], F32)
    inp("bet_fm", [128, 24], F32)
    inp("laT", [D, 192], F32)
    inp("lbT", [64, 3 * D], F32)
    inp("ident", [128, 128], BF16)
    inp("diagmask", [128, 128], BF16)
    inp("selhalf", [128, 2], BF16)
    inp("selpair_f", [2, 128], F32)
    inp("selrq_f", [2, 128], F32)
    inp("ones_1x128_b", [1, 128], BF16)
    inp("nones_1x128_b", [1, 128], BF16)
    inp("ones_1x128_f", [1, 128], F32)
    inp("ones_c128_b", [128, 1], BF16)
    inp("ones_c128_f", [128, 1], F32)
    inp("ones_row_b", [1, C], BF16)
    inp("onescol3", [128, 3], BF16)

    xn_scr = nc.dram_tensor("xn_scr", [D, T], F32, kind="Internal").ap()
    s_scr = nc.dram_tensor("s_scr", [D, T], F32, kind="Internal").ap()
    w_scr = nc.dram_tensor("w_scr", [6 * D, D], BF16, kind="Internal").ap()
    outT = nc.dram_tensor("outT", [D, T], F32, kind="ExternalOutput").ap()
    tap_ap = {}
    for tname, tshape in [("qf0", [128, NDC * C]), ("kf0", [128, NDC * C]),
                          ("v0", [128, 4 * 1040]), ("attn0", [128, NDC * C]),
                          ("y0", [128, NDC * C]), ("g10", [128, NDC * C]),
                          ("h1", [128, NDC * C]), ("u0", [64, C]),
                          ("mu0", [1, C]), ("rs0", [1, C]),
                          ("h2", [128, NDC * C]), ("h3", [128, NDC * C]),
                          ("gg0", [128, C]), ("y1", [128, NDC * C]),
                          ("g11", [128, NDC * C])]:
        if tname in taps:
            tap_ap[tname] = nc.dram_tensor(tname, tshape, F32,
                                           kind="ExternalOutput").ap()

    def tap(tname, buf, cond=True):
        if tname in tap_ap and cond:
            k.dma("gpsimd", "sg_tap",
                  lambda e, d=tap_ap[tname], s=buf.ap: e.dma_start(out=d, in_=s),
                  ins=[buf])

    B = [k.bank(f"pb{i}") for i in range(7)]
    PT = k.bank("pbt", BF16)
    es1 = contextlib.ExitStack()

    ident_sb = k.sb("ident_sb", [128, 128], BF16)
    selhalf_sb = k.sb("selhalf_sb", [128, 2], BF16)
    selpair_sb = k.sb("selpair_sb", [2, 128], F32)
    selrq_sb = k.sb("selrq_sb", [2, 128], F32)
    ones1x128b = k.sb("ones1x128b", [1, 128], BF16)
    nones1x128b = k.sb("nones1x128b", [1, 128], BF16)
    ones1x128f = k.sb("ones1x128f", [1, 128], F32)
    onesc128b = k.sb("onesc128b", [128, 1], BF16)
    onesc128f = k.sb("onesc128f", [128, 1], F32)
    onesrowb = k.sb("onesrowb", [1, C], BF16)
    onescol3 = k.sb("onescol3_sb", [128, 3], BF16)
    bout_sb = k.sb("bout_sb", [128, NDC], F32)
    bcol_sb = k.sb("bcol_sb", [128, 48], F32)
    gam_sb = k.sb("gam_sb", [128, 24], F32)
    bet_sb = k.sb("bet_sb", [128, 24], F32)
    laT_sb = k.sb("laT_sb", [128, NDC * 192], BF16)
    lbT_sb = k.sb("lbT_sb", [64, 3 * D], BF16)
    wbar_sb = k.sb("wbar_sb", [128, 48], BF16)
    lbar_f = k.sb("lbar_f", [64, 3], F32)
    lbar_sb = k.sb("lbar_sb", [64, 3], BF16)
    bsum_col = k.sb("bsum_col", [128, 6], F32)
    bbar_sb = k.sb("bbar_sb", [1, 6], BF16)
    hi_col = k.sb("hi_col", [128, 1], F32)
    lo_col = k.sb("lo_col", [128, 1], F32)
    lncs_col = k.sb("lncs_col", [128, 1], F32)
    scol_sb = k.sb("scol_sb", [128, 1], F32)
    qn_sb = k.sb("qn_sb", [2, 1], F32)
    qw_sb = k.sb("qw_sb", [2, 1], F32)
    su = [k.sb(f"su{i}", [2, 1], F32) for i in range(4)]
    st_var = k.sb("st_var", [1, C], F32)
    st_ln = k.sb("st_ln", [1, C], F32)
    st_rs = k.sb("st_rs", [1, C], F32)

    wqkv_sb = k.sb("wqkv_sb", [128, NDC * 3 * D], BF16, es=es1)
    wout_sb = k.sb("wout_sb", [128, NDC * D], BF16, es=es1)
    omg_sb = k.sb("omg_sb", [128, 8 * 128], BF16, es=es1)
    dmask_sb = k.sb("dmask_sb", [128, 128], BF16, es=es1)
    bqkv_row = k.sb("bqkv_row", [1, 2 * D], BF16, es=es1)
    bv_row_sb = k.sb("bv_row_sb", [1, D], BF16, es=es1)
    s2s_row = k.sb("s2s_row", [1, 128], F32, es=es1)
    s2s_2 = k.sb("s2s_2", [2, 128], F32, es=es1)
    sel2s = k.sb("sel2s", [2, 128], F32, es=es1)
    S_sb = [k.sb(f"S_sb{p}", [128, 8 * 65], F32, es=es1) for p in range(2)]
    SZ_bf = k.sb("SZ_bf", [128, 65], BF16, es=es1)
    zq_sb = k.sb("zq_sb", [128, 4], BF16, es=es1)
    def ldc(dst_buf, src_ap, dst_ap=None):
        da = dst_ap if dst_ap is not None else dst_buf.ap
        return k.dma("sync", "si_m",
                     lambda e, d=da, s=src_ap: e.dma_start(out=d, in_=s),
                     out=dst_buf)

    def ldg(dst_buf, src_ap, dst_ap=None, sem="sg_su"):
        da = dst_ap if dst_ap is not None else dst_buf.ap
        return k.dma("gpsimd", sem,
                     lambda e, d=da, s=src_ap: e.dma_start(out=d, in_=s),
                     out=dst_buf)

    for b_, nm in [(ident_sb, "ident"), (dmask_sb, "diagmask"),
                   (selhalf_sb, "selhalf"), (selpair_sb, "selpair_f"),
                   (selrq_sb, "selrq_f"), (ones1x128b, "ones_1x128_b"),
                   (nones1x128b, "nones_1x128_b"), (ones1x128f, "ones_1x128_f"),
                   (onesc128b, "ones_c128_b"), (onesc128f, "ones_c128_f"),
                   (onesrowb, "ones_row_b"), (onescol3, "onescol3"),
                   (bout_sb, "bout_fm"), (bcol_sb, "bcol_fm"),
                   (gam_sb, "gam_fm"), (bet_sb, "bet_fm"),
                   (qn_sb, "qn2"), (qw_sb, "qw2")]:
        ldc(b_, ap[nm])
    ldg(omg_sb, ap["omgblk"])
    ldg(bqkv_row, ap["bq_row"], dst_ap=bqkv_row[0:1, 0:D])
    ldg(bqkv_row, ap["bk_row"], dst_ap=bqkv_row[0:1, D:2 * D])
    ldg(bv_row_sb, ap["bv_row"])
    for dc in range(NDC):
        r = slice(128 * dc, 128 * dc + 128)
        ldg(wqkv_sb, ap["wqkv"][r, :],
            dst_ap=wqkv_sb[:, 3 * D * dc:3 * D * (dc + 1)])
        ldg(wout_sb, ap["wout"][r, :], dst_ap=wout_sb[:, D * dc:D * (dc + 1)])
        ldg(laT_sb, ap["laT"][r, :], dst_ap=laT_sb[:, 192 * dc:192 * (dc + 1)])
    ldg(lbT_sb, ap["lbT"])

    k.op("vector", lambda v: v.tensor_scalar(
        out=su[0][0:2, :], in0=qn_sb[0:2, :], scalar1=0.0, scalar2=2.0,
        op0=ALU.max, op1=ALU.mult), ins=[qn_sb], out=su[0])
    k.op("scalar", lambda s: s.activation(out=su[1][0:2, :], in_=su[0][0:2, :],
                                          func=ACTF.Ln), ins=[su[0]], out=su[1])
    k.op("scalar", lambda s: s.activation(out=su[2][0:2, :], in_=su[1][0:2, :],
                                          func=ACTF.Exp, scale=0.5),
         ins=[su[1]], out=su[2])
    k.mm(B[0], B[0].ap[0:1, 0:128], su[2].ap[0:2, 0:1], selrq_sb.ap, True, True,
         ins=[su[2], selrq_sb])
    tok = k.op("vector", lambda v: v.tensor_copy(out=s2s_row.ap,
                                                 in_=B[0].ap[0:1, 0:128]),
               ins=[B[0]], out=s2s_row)
    B[0].rtoks.append(tok)
    k.mm(B[1], B[1].ap[0:2, 0:128], ones1x128f.ap[0:1, 0:2], s2s_row.ap,
         True, True, ins=[ones1x128f, s2s_row])
    tok = k.op("vector", lambda v: v.tensor_copy(out=s2s_2.ap,
                                                 in_=B[1].ap[0:2, 0:128]),
               ins=[B[1]], out=s2s_2)
    B[1].rtoks.append(tok)
    k.op("vector", lambda v: v.tensor_tensor(
        out=sel2s.ap, in0=selpair_sb.ap, in1=s2s_2.ap, op=ALU.mult),
        ins=[selpair_sb, s2s_2], out=sel2s)
    k.mm(B[2], B[2].ap[:, 0:1], selrq_sb.ap, qn_sb.ap[0:2, :], True, True,
         ins=[selrq_sb, qn_sb])
    tok = k.op("vector", lambda v: v.tensor_copy(out=scol_sb.ap,
                                                 in_=B[2].ap[:, 0:1]),
               ins=[B[2]], out=scol_sb)
    B[2].rtoks.append(tok)
    k.op("vector", lambda v: v.tensor_scalar(
        out=hi_col.ap, in0=scol_sb.ap, scalar1=20.0, scalar2=None, op0=ALU.add),
        ins=[scol_sb], out=hi_col)
    k.op("vector", lambda v: v.tensor_scalar(
        out=lo_col.ap, in0=scol_sb.ap, scalar1=-20.0, scalar2=None, op0=ALU.add),
        ins=[scol_sb], out=lo_col)
    k.op("vector", lambda v: v.tensor_scalar(
        out=su[3][0:2, :], in0=qw_sb[0:2, :], scalar1=0.0, scalar2=None,
        op0=ALU.max), ins=[qw_sb], out=su[3])
    k.op("scalar", lambda s: s.activation(out=su[0][0:2, :], in_=su[3][0:2, :],
                                          func=ACTF.Ln), ins=[su[3]], out=su[0])
    k.op("vector", lambda v: v.tensor_scalar(
        out=su[1][0:2, :], in0=su[0][0:2, :], scalar1=0.5,
        scalar2=-0.5 * float(np.log(32.0)), op0=ALU.mult, op1=ALU.add),
        ins=[su[0]], out=su[1])
    k.mm(B[3], B[3].ap[:, 0:1], selrq_sb.ap, su[1].ap[0:2, :], True, True,
         ins=[selrq_sb, su[1]])
    tok = k.op("vector", lambda v: v.scalar_tensor_tensor(
        out=lncs_col.ap, in0=B[3].ap[:, 0:1], scalar=1.0, in1=scol_sb.ap,
        op0=ALU.bypass, op1=ALU.subtract), ins=[B[3], scol_sb], out=lncs_col)
    B[3].rtoks.append(tok)
    for l in range(6):
        k.op("vector", lambda v, l=l: v.tensor_reduce(
            out=bsum_col[:, l:l + 1], in_=bcol_sb[:, 8 * l:8 * l + 8],
            axis=AXX, op=ALU.add), ins=[bcol_sb], out=bsum_col)
    k.mm(B[4], B[4].ap[0:1, 0:6], onesc128f.ap, bsum_col.ap, True, True,
         ins=[onesc128f, bsum_col])
    tok = k.op("vector", lambda v: v.tensor_copy(out=bbar_sb.ap,
                                                 in_=B[4].ap[0:1, 0:6]),
               ins=[B[4]], out=bbar_sb)
    B[4].rtoks.append(tok)
    for b_ in range(3):
        k.op("vector", lambda v, b_=b_: v.tensor_reduce(
            out=lbar_f[0:64, b_:b_ + 1], in_=lbT_sb[0:64, D * b_:D * (b_ + 1)],
            axis=AXX, op=ALU.add), ins=[lbT_sb], out=lbar_f)
    k.op("vector", lambda v: v.tensor_copy(out=lbar_sb.ap, in_=lbar_f.ap),
         ins=[lbar_f], out=lbar_sb)

    x_sb = k.sb("x_sb", [128, NDC * C], F32, es=es1)
    xb_buf = [k.sb("xb0", [128, C], BF16, es=es1)] * 2
    sqx_buf = [k.sb("sqx0", [128, C], BF16, es=es1)] * 2
    t1_buf = [k.sb("t1_0", [128, C], F32, es=es1)] * 2
    xnf_buf = [k.sb(f"xnf{i}", [128, C], F32, es=es1) for i in range(2)]
    xn_bf = k.sb("xn_bf", [128, NDC * C], BF16, es=es1)
    xnv = K.views(xn_bf, NDC, C)
    st_mu = k.sb("st_mu", [1, C], F32, es=es1)
    st_msq = k.sb("st_msq", [1, C], F32, es=es1)
    st_nmu = k.sb("st_nmu", [1, C], BF16, es=es1)
    qb_buf = [k.sb(f"qb{i}", [128, C], BF16, es=es1) for i in range(4)]
    sq_buf = [k.sb(f"sq{i}", [128, C], BF16, es=es1) for i in range(2)]
    s2_sb = [k.sb(f"s2_{i}", [2, C], F32, es=es1) for i in range(4)]
    rln_sb = [k.sb("rln0", [2, C], F32, es=es1)]
    rex_sb = [k.sb("rex0", [2, C], F32, es=es1)]
    rmin_sb = rln_sb
    rbs_sb = [k.sb(f"rbs{i}", [128, C], F32, es=es1) for i in range(2)]
    y_buf = [k.sb("yb0", [128, C], F32, es=es1)] * 2
    c2_buf = [k.sb("c2_0", [128, C], F32, es=es1)] * 2
    qf_sb = k.sb("qf_sb", [128, NDC * C], BF16, es=es1)
    kf_sb = k.sb("kf_sb", [128, NDC * C], BF16, es=es1)
    qfv = K.views(qf_sb, NDC, C)
    kfv = K.views(kf_sb, NDC, C)
    attnv = qfv
    v_sb = k.sb("v_sb", [128, 4 * 1040], BF16, es=es1)
    vv = K.views(v_sb, 4, 1040)
    kfT_sb = k.sb("kfT_sb", [128, 512], BF16, es=es1)
    atm_sb = k.sb("atm_sb", [128, 2 * 1280], BF16, es=es1)
    den_sb = k.sb("den_sb", [2, C], F32, es=es1)
    dln_sb = k.sb("dln_sb", [2, C], F32, es=es1)
    recip_sb = den_sb
    ctx_sb = k.sb("ctx_sb", [128, C], F32, es=es1)
    sout_buf = [k.sb(f"sout{i}", [128, C], F32, es=es1) for i in range(2)]
    dq_qi = [k.sb("dqqi0", [128, 512], I32, es=es1)]
    dq_sc = [k.sb("dqsc0", [128, 512], F32, es=es1)]
    dq_t = [k.sb("dqt0", [128, 512], F32, es=es1)]
    dq_w = [k.sb("dqw0", [128, 512], BF16, es=es1)]

    for tt in range(4):
        v65 = vv[tt].ap.rearrange("p (a b) -> p a b", b=65)
        k.op("gpsimd", lambda g, v65=v65: g.memset(v65[:, :, 64:65], 1.0),
             out=vv[tt])
    k.op("gpsimd", lambda g: g.memset(zq_sb[:, 1:3], 0.0), out=zq_sb)
    k.op("gpsimd", lambda g: g.memset(S_sb[0].ap, 0.0), out=S_sb[0])

    for idx in range(96):
        half = idx % 2
        l, dc = divmod(idx // 2, NDC)
        sl = 0
        r = slice(D * l + 128 * dc, D * l + 128 * dc + 128)
        cseg = slice(512 * half, 512 * half + 512)
        ldg(dq_qi[sl], ap["qT"][r, cseg], sem="sg")
        ldg(dq_sc[sl], ap["scT"][r, cseg], sem="sg")
        k.op("gpsimd", lambda g, sl=sl: g.tensor_scalar(
            out=dq_t[sl].ap, in0=dq_qi[sl].ap, scalar1=1.0 / 7.5, scalar2=-1.0,
            op0=ALU.mult, op1=ALU.add), ins=[dq_qi[sl]], out=dq_t[sl])
        k.op("gpsimd", lambda g, sl=sl: g.tensor_tensor(
            out=dq_w[sl].ap, in0=dq_t[sl].ap, in1=dq_sc[sl].ap, op=ALU.mult),
            ins=[dq_t[sl], dq_sc[sl]], out=dq_w[sl])
        k.dma("gpsimd", "sg",
              lambda e, sl=sl, r=r, cseg=cseg: e.dma_start(
                  out=w_scr[r, cseg], in_=dq_w[sl].ap), ins=[dq_w[sl]])


    def feat_s1(oc, bank, which="q"):
        qb = qb_buf[2 * (oc % 2) + (0 if which == "q" else 1)]
        sl = 2 * (oc % 2) + (0 if which == "q" else 1)
        tok = k.op("scalar", lambda s, qb=qb, bank=bank: s.copy(
            out=qb.ap, in_=bank.ap), ins=[bank], out=qb)
        bank.rtoks.append(tok)
        k.op("vector", lambda v, qb=qb, sl=sl: v.tensor_tensor(
            out=sq_buf[sl % 2].ap, in0=qb.ap, in1=qb.ap, op=ALU.mult),
            ins=[qb], out=sq_buf[sl % 2])
        k.mm(B[4], B[4].ap[0:2, :], selhalf_sb.ap, sq_buf[sl % 2].ap,
             True, True, ins=[selhalf_sb, sq_buf[sl % 2]])
        tok = k.op("vector", lambda v, sl=sl: v.tensor_copy(
            out=s2_sb[sl].ap, in_=B[4].ap[0:2, :]), ins=[B[4]], out=s2_sb[sl])
        B[4].rtoks.append(tok)
        return qb

    def feat_s2(oc, bank, qb, dstbuf, which="q"):
        sl = 2 * (oc % 2) + (0 if which == "q" else 1)
        k.op("scalar", lambda s, sl=sl: s.activation(
            out=rln_sb[0].ap, in_=s2_sb[sl].ap, func=ACTF.Ln),
            ins=[s2_sb[sl]], out=rln_sb[0])
        k.op("scalar", lambda s: s.activation(
            out=rex_sb[0].ap, in_=rln_sb[0].ap, func=ACTF.Exp, scale=-0.5),
            ins=[rln_sb[0]], out=rex_sb[0])
        k.op("vector", lambda v: v.tensor_scalar(
            out=rmin_sb[0].ap, in0=rex_sb[0].ap, scalar1=1e12,
            scalar2=None, op0=ALU.min), ins=[rex_sb[0]], out=rmin_sb[0])
        k.mm(B[5], B[5].ap[:, :], sel2s.ap, rmin_sb[0].ap, True, True,
             ins=[sel2s, rmin_sb[0]])
        tok = k.op("vector", lambda v: v.tensor_copy(
            out=rbs_sb[0].ap, in_=B[5].ap), ins=[B[5]], out=rbs_sb[0])
        B[5].rtoks.append(tok)
        k.mm(B[6], B[6].ap[:, :], omg_sb[:, 128 * oc:128 * oc + 128],
             qb.ap, True, True, ins=[omg_sb, qb])
        tok = k.op("vector", lambda v: v.scalar_tensor_tensor(
            out=y_buf[0].ap, in0=rbs_sb[0].ap, scalar=1.0, in1=B[6].ap,
            op0=ALU.bypass, op1=ALU.mult), ins=[rbs_sb[0], B[6]],
            out=y_buf[0])
        B[6].rtoks.append(tok)
        k.op("vector", lambda v: v.tensor_scalar(
            out=c2_buf[0].ap, in0=y_buf[0].ap, scalar1=hi_col.ap,
            scalar2=lo_col.ap, op0=ALU.min, op1=ALU.max),
            ins=[y_buf[0], hi_col, lo_col], out=c2_buf[0])
        k.op("scalar", lambda s, dstbuf=dstbuf: s.activation(
            out=dstbuf.ap, in_=c2_buf[0].ap,
            func=ACTF.Exp, bias=lncs_col.ap),
            ins=[c2_buf[0], lncs_col], out=dstbuf)

    for c in range(NCH):
        cs = slice(C * c, C * c + C)
        pcur, pnew = c % 2, 1 - (c % 2)
        Scur, Snew = S_sb[pcur], S_sb[pnew]
        for dc in range(NDC):
            k.dma("sync", "si_x",
                  lambda e, dc=dc, cs=cs: e.dma_start(
                      out=x_sb[:, C * dc:C * dc + C],
                      in_=ap["xT"][128 * dc:128 * dc + 128, cs]),
                  out=x_sb, waw=False)
        for dc in range(NDC):
            sl = dc % 2
            xd = x_sb[:, C * dc:C * dc + C]
            k.op("scalar", lambda s, xd=xd, sl=sl: s.copy(
                out=xb_buf[sl].ap, in_=xd), ins=[x_sb], out=xb_buf[sl])
            k.op("vector", lambda v, xd=xd, sl=sl: v.tensor_tensor(
                out=sqx_buf[sl].ap, in0=xd, in1=xd, op=ALU.mult),
                ins=[x_sb], out=sqx_buf[sl])
            k.mm(B[5], B[5].ap[0:1, :], onesc128b.ap, xb_buf[sl].ap,
                 dc == 0, dc == NDC - 1, ins=[onesc128b, xb_buf[sl]])
            k.mm(B[6], B[6].ap[0:1, :], onesc128b.ap, sqx_buf[sl].ap,
                 dc == 0, dc == NDC - 1, ins=[onesc128b, sqx_buf[sl]])
        tok = k.op("vector", lambda v: v.tensor_scalar(
            out=st_mu.ap, in0=B[5].ap[0:1, :], scalar1=1.0 / 1024.0,
            scalar2=None, op0=ALU.mult), ins=[B[5]], out=st_mu)
        B[5].rtoks.append(tok)
        k.op("vector", lambda v: v.tensor_tensor(
            out=st_msq.ap, in0=st_mu.ap, in1=st_mu.ap, op=ALU.mult),
            ins=[st_mu], out=st_msq)
        tok = k.op("vector", lambda v: v.scalar_tensor_tensor(
            out=st_var.ap, in0=B[6].ap[0:1, :], scalar=1.0 / 1024.0,
            in1=st_msq.ap, op0=ALU.mult, op1=ALU.subtract),
            ins=[B[6], st_msq], out=st_var)
        B[6].rtoks.append(tok)
        k.op("vector", lambda v: v.tensor_scalar(
            out=st_ln.ap, in0=st_var.ap, scalar1=1e-5, scalar2=None,
            op0=ALU.add), ins=[st_var], out=st_ln)
        k.op("scalar", lambda s: s.activation(out=st_var.ap, in_=st_ln.ap,
                                              func=ACTF.Ln),
             ins=[st_ln], out=st_var)
        k.op("scalar", lambda s: s.activation(out=st_rs.ap, in_=st_var.ap,
                                              func=ACTF.Exp, scale=-0.5),
             ins=[st_var], out=st_rs)
        k.op("vector", lambda v: v.tensor_scalar(
            out=st_nmu.ap, in0=st_mu.ap, scalar1=-1.0, scalar2=None,
            op0=ALU.mult), ins=[st_mu], out=st_nmu)
        k.mm(B[5], B[5].ap[:, :], ones1x128b.ap, st_nmu.ap, True, True,
             ins=[ones1x128b, st_nmu])
        k.mm(B[6], B[6].ap[:, :], ones1x128f.ap, st_rs.ap, True, True,
             ins=[ones1x128f, st_rs])
        for dc in range(NDC):
            sl = dc % 2
            xd = x_sb[:, C * dc:C * dc + C]
            t_a = k.op("vector", lambda v, xd=xd, sl=sl: v.tensor_tensor(
                out=t1_buf[sl].ap, in0=xd, in1=B[5].ap, op=ALU.add),
                ins=[x_sb, B[5]], out=t1_buf[sl])
            B[5].rtoks.append(t_a)
            t_b = k.op("vector", lambda v, sl=sl: v.tensor_tensor(
                out=xnf_buf[sl].ap, in0=t1_buf[sl].ap, in1=B[6].ap,
                op=ALU.mult), ins=[t1_buf[sl], B[6]], out=xnf_buf[sl])
            B[6].rtoks.append(t_b)
            k.op("scalar", lambda s, dc=dc, sl=sl: s.copy(
                out=xnv[dc].ap, in_=xnf_buf[sl].ap),
                ins=[xnf_buf[sl]], out=xnv[dc])
            k.dma("sync", "si_sc",
                  lambda e, dc=dc, cs=cs, sl=sl: e.dma_start(
                      out=xn_scr[128 * dc:128 * dc + 128, cs],
                      in_=xnf_buf[sl].ap), ins=[xnf_buf[sl]])

        def qk_mms(oc):
            bq = B[oc % 2]
            bk = B[2 + oc % 2]
            for dc in range(NDC):
                k.mm(bq, bq.ap[:, :],
                     wqkv_sb[:, 3 * D * dc + 128 * oc:
                             3 * D * dc + 128 * oc + 128],
                     xnv[dc].ap, dc == 0, False,
                     ins=[wqkv_sb, xnv[dc]])
            k.mm(bq, bq.ap[:, :], bqkv_row[0:1, 128 * oc:128 * oc + 128],
                 onesrowb.ap, False, True, ins=[bqkv_row, onesrowb])
            for dc in range(NDC):
                k.mm(bk, bk.ap[:, :],
                     wqkv_sb[:, 3 * D * dc + D + 128 * oc:
                             3 * D * dc + D + 128 * oc + 128],
                     xnv[dc].ap, dc == 0, False,
                     ins=[wqkv_sb, xnv[dc]])
            k.mm(bk, bk.ap[:, :], bqkv_row[0:1, D + 128 * oc:D + 128 * oc + 128],
                 onesrowb.ap, False, True, ins=[bqkv_row, onesrowb])
            return bq, bk

        st1 = {}
        st2 = {}
        for oc in range(NDC + 2):
            if oc < NDC:
                bq, bk = qk_mms(oc)
                st1[oc] = (bq, bk)
            if oc - 1 in st1:
                o1 = oc - 1
                bq1, bk1 = st1.pop(o1)
                qbq = feat_s1(o1, bq1, "q")
                qbk = feat_s1(o1, bk1, "k")
                st2[o1] = (qbq, qbk, bq1, bk1)
            if oc - 2 in st2:
                o2 = oc - 2
                qbq, qbk, bq2, bk2 = st2.pop(o2)
                feat_s2(o2, bq2, qbq, qfv[o2], "q")
                feat_s2(o2, bk2, qbk, kfv[o2], "k")

        for tt in range(4):
            for half in range(2):
                bank = B[5 + half]
                for dc in range(NDC):
                    k.mm(bank, bank.ap[:, :],
                         xnv[dc].ap[:, 128 * tt:128 * tt + 128],
                         wqkv_sb[:, 3 * D * dc + 2 * D + C * half:
                                 3 * D * dc + 2 * D + C * half + C],
                         dc == 0, False, ins=[xnv[dc], wqkv_sb])
                k.mm(bank, bank.ap[:, :], ones1x128b.ap,
                     bv_row_sb[0:1, C * half:C * half + C], False, True,
                     ins=[ones1x128b, bv_row_sb])
                src = bank.ap.rearrange("p (h w) -> p h w", w=64)
                dst = vv[tt].ap[:, 520 * half:520 * half + 520]
                dst = dst.rearrange("p (h w) -> p h w", w=65)[:, :, 0:64]
                tok = k.op("vector", lambda v, d=dst, s=src: v.tensor_copy(
                    out=d, in_=s), ins=[bank], out=vv[tt])
                bank.rtoks.append(tok)

        for i in range(NDC):
            k.op("vector", lambda v, i=i, Scur=Scur: v.tensor_copy(
                out=SZ_bf.ap, in_=Scur.ap[:, 65 * i:65 * i + 65]),
                ins=[Scur], out=SZ_bf)
            k.op("vector", lambda v, i=i, Scur=Scur: v.tensor_copy(
                out=zq_sb[:, 0:1], in_=Scur.ap[:, 65 * i + 64:65 * i + 65]),
                ins=[Scur], out=zq_sb)
            k.op("vector", lambda v, i=i, Scur=Scur: v.tensor_copy(
                out=zq_sb[:, 3:4], in_=Scur.ap[:, 65 * i + 64:65 * i + 65]),
                ins=[Scur], out=zq_sb)
            for j in range(4):
                k.mm(PT, PT.ap[:, 128 * j:128 * j + 128],
                     kfv[i].ap[:, 128 * j:128 * j + 128],
                     ident_sb.ap, j == 0, j == 3,
                     ins=[kfv[i], ident_sb], is_transpose=True)
            tok = k.op("vector", lambda v: v.tensor_copy(
                out=kfT_sb.ap, in_=PT.ap[:, 0:512]), ins=[PT], out=kfT_sb)
            PT.rtoks.append(tok)
            for par in range(2):
                qr = slice(64 * par, 64 * par + 64)
                for j in range(4):
                    bank = B[par + 2 * (j % 2)]
                    k.mm(bank, bank.ap[:, :],
                         kfv[i].ap[qr, 128 * j:128 * j + 128],
                         qfv[i].ap[qr, :], True, True,
                         ins=[kfv[i], qfv[i]])
                    base = 1280 * par + ATM_OFF[j]
                    t1 = k.op("vector", lambda v, bank=bank, base=base, j=j:
                              v.tensor_tensor(
                                  out=atm_sb[:, base:base + 128],
                                  in0=bank.ap[:, 128 * j:128 * j + 128],
                                  in1=dmask_sb.ap, op=ALU.mult),
                              ins=[bank, dmask_sb], out=atm_sb)
                    bank.rtoks.append(t1)
                    if j < 3:
                        w = ATM_W[j] - 128
                        t2 = k.op("scalar", lambda s, bank=bank, base=base,
                                  j=j, w=w: s.copy(
                                      out=atm_sb[:, base + 128:base + 128 + w],
                                      in_=bank.ap[:, 128 * (j + 1):C]),
                                  ins=[bank], out=atm_sb)
                        bank.rtoks.append(t2)
            for par in range(2):
                h = 2 * i + par
                qr = slice(64 * par, 64 * par + 64)
                k.mm(B[4], B[4].ap[qr, :],
                     SZ_bf.ap[qr, 0:64], qfv[i].ap[qr, :],
                     True, False, ins=[SZ_bf, qfv[i]])
                for j in range(4):
                    k.mm(B[4], B[4].ap[qr, 128 * j:C],
                         vv[j].ap[:, 65 * h:65 * h + 64],
                         atm_sb.ap[:, 1280 * par + ATM_OFF[j]:
                                   1280 * par + ATM_OFF[j] + ATM_W[j]],
                         False, par == 1 and j == 3, ins=[vv[j], atm_sb])
            for par in range(2):
                qr = slice(64 * par, 64 * par + 64)
                zsl = (slice(0, 64), slice(0, 2)) if par == 0 else \
                    (slice(64, 128), slice(2, 4))
                k.mm(B[5], B[5].ap[0:2, :],
                     zq_sb.ap[zsl[0], zsl[1]], qfv[i].ap[qr, :],
                     par == 0, False, ins=[zq_sb, qfv[i]])
                for j in range(4):
                    k.mm(B[5], B[5].ap[0:2, 128 * j:C],
                         onescol3.ap[:, par:par + 2],
                         atm_sb.ap[:, 1280 * par + ATM_OFF[j]:
                                   1280 * par + ATM_OFF[j] + ATM_W[j]],
                         False, par == 1 and j == 3, ins=[onescol3, atm_sb])
            for par in range(2):
                h = 2 * i + par
                for j in range(4):
                    k.mm(B[6], B[6].ap[64 * par:64 * par + 64, 0:65],
                         kfT_sb.ap[:, 128 * j + 64 * par:
                                   128 * j + 64 * par + 64],
                         vv[j].ap[:, 65 * h:65 * h + 65],
                         j == 0, par == 1 and j == 3, ins=[kfT_sb, vv[j]])
            tok = k.op("vector", lambda v: v.tensor_scalar(
                out=den_sb.ap, in0=B[5].ap[0:2, :], scalar1=1e-6,
                scalar2=None, op0=ALU.max), ins=[B[5]], out=den_sb)
            B[5].rtoks.append(tok)
            k.op("scalar", lambda s: s.activation(out=dln_sb.ap,
                                                  in_=den_sb.ap, func=ACTF.Ln),
                 ins=[den_sb], out=dln_sb)
            k.op("scalar", lambda s: s.activation(out=recip_sb.ap,
                                                  in_=dln_sb.ap,
                                                  func=ACTF.Exp, scale=-1.0),
                 ins=[dln_sb], out=recip_sb)
            k.mm(B[5], B[5].ap[:, :], selpair_sb.ap, recip_sb.ap, True, True,
                 ins=[selpair_sb, recip_sb])
            tok = k.op("vector", lambda v, i=i, Scur=Scur, Snew=Snew:
                       v.tensor_tensor(
                           out=Snew.ap[:, 65 * i:65 * i + 65],
                           in0=B[6].ap[:, 0:65],
                           in1=Scur.ap[:, 65 * i:65 * i + 65], op=ALU.add),
                       ins=[B[6], Scur], out=Snew)
            B[6].rtoks.append(tok)
            tok = k.op("vector", lambda v: v.tensor_copy(
                out=ctx_sb.ap, in_=B[4].ap), ins=[B[4]], out=ctx_sb)
            B[4].rtoks.append(tok)
            tok = k.op("vector", lambda v, i=i: v.tensor_tensor(
                out=attnv[i].ap, in0=ctx_sb.ap, in1=B[5].ap,
                op=ALU.mult), ins=[ctx_sb, B[5]], out=attnv[i])
            B[5].rtoks.append(tok)

        for oc in range(NDC):
            bank = B[oc % 2]
            for dc in range(NDC):
                k.mm(bank, bank.ap[:, :],
                     wout_sb[:, D * dc + 128 * oc:D * dc + 128 * oc + 128],
                     attnv[dc].ap, dc == 0, dc == NDC - 1,
                     ins=[wout_sb, attnv[dc]])
            sl = oc % 2
            tok = k.op("scalar", lambda s, bank=bank, oc=oc, sl=sl:
                       s.activation(out=sout_buf[sl].ap, in_=bank.ap,
                                    func=ACTF.Identity,
                                    bias=bout_sb.ap[:, oc:oc + 1]),
                       ins=[bank, bout_sb], out=sout_buf[sl])
            bank.rtoks.append(tok)
            k.dma("sync", "si_sc",
                  lambda e, oc=oc, cs=cs, sl=sl: e.dma_start(
                      out=s_scr[128 * oc:128 * oc + 128, cs],
                      in_=sout_buf[sl].ap), ins=[sout_buf[sl]])

    pass1_toks = [(s, k.cnt[s]) for s in ("e_t", "e_v", "e_s", "e_g",
                                          "si_x", "si_sc", "sg", "sg_su",
                                          "si_m")]
    barrier_toks = []
    for ename in ("tensor", "vector", "scalar", "gpsimd", "sync"):
        tok = k._emit(ename, lambda e: e.nop(), pass1_toks, ESEM[ename], 1)
        barrier_toks.append(tok)
    for bk in B + [PT]:
        bk.rtoks = list(barrier_toks)
        bk.wtok = None
    es1.close()

    es2 = contextlib.ExitStack()
    h_sb = [k.sb(f"h_sb{p}", [128, NDC * C], F32, es=es2) for p in range(2)]
    hv = [K.views(h_sb[p], NDC, C) for p in range(2)]
    h_bf = k.sb("h_bf", [128, NDC * C], BF16, es=es2)
    hbv = K.views(h_bf, NDC, C)
    g1_bf = k.sb("g1_bf", [128, NDC * C], BF16, es=es2)
    g1v = K.views(g1_bf, NDC, C)
    y_sb = k.sb("y_sb", [128, NDC * C], F32, es=es2)
    yv = K.views(y_sb, NDC, C)
    wfull = [k.sb(f"wfull{p}", [128, NDC * D], BF16, es=es2) for p in range(3)]
    u_bf = k.sb("u_bf", [64, C], BF16, es=es2)
    mub_sb = k.sb("mub_sb", [1, C], BF16, es=es2)
    sqy_buf = [k.sb("sqy0", [128, C], BF16, es=es2)] * 2
    t4_buf = [k.sb("t4_0", [128, C], F32, es=es2)] * 2
    t5_buf = [k.sb("t5_0", [128, C], F32, es=es2)] * 2
    sg_buf = [k.sb("sgb0", [128, C], F32, es=es2)] * 2
    gg_buf = [k.sb("ggb0", [128, C], F32, es=es2)] * 2
    e2_buf = sg_buf
    s_in = [k.sb(f"s_in{i}", [128, C], F32, es=es2) for i in range(2)]
    x_in = [k.sb(f"x_in{i}", [128, C], F32, es=es2) for i in range(2)]
    o_buf = [k.sb(f"o_buf{i}", [128, C], F32, es=es2) for i in range(2)]
    wbar_t = k.sb("wbar_t", [128, 1], F32, es=es2)

    def load_w(l, slot, extra=()):
        for dc in range(NDC):
            r = slice(D * l + 128 * dc, D * l + 128 * dc + 128)
            k.dma("sync", "si_w",
                  lambda e, r=r, dc=dc, slot=slot: e.dma_start(
                      out=wfull[slot][:, D * dc:D * dc + D], in_=w_scr[r, :]),
                  ins=list(extra), out=wfull[slot], waw=False)

    sg_done = ("sg", k.cnt["sg"])
    load_w(0, 0, extra=[sg_done])
    load_w(1, 1, extra=[sg_done])
    load_w(2, 2, extra=[sg_done])

    for c in range(NCH):
        cs = slice(C * c, C * c + C)
        for dc in range(NDC):
            k.dma("sync", "si_x",
                  lambda e, dc=dc, cs=cs: e.dma_start(
                      out=hv[0][dc].ap,
                      in_=xn_scr[128 * dc:128 * dc + 128, cs]),
                  out=hv[0][dc])
        for dc in range(NDC):
            k.op("scalar", lambda s, dc=dc: s.copy(
                out=hbv[dc].ap, in_=hv[0][dc].ap),
                ins=[hv[0][dc]], out=hbv[dc])
        for blk in range(3):
            lA, lB = 2 * blk, 2 * blk + 1
            wA, wB = lA % 3, lB % 3
            hcurv, hnewv = hv[blk % 2], hv[1 - blk % 2]
            ur = slice(0, 32)
            for dc in range(NDC):
                k.mm(B[4], B[4].ap[ur, :],
                     laT_sb[:, 192 * dc + 32 * lA:192 * dc + 32 * lA + 32],
                     hbv[dc].ap, dc == 0, dc == NDC - 1,
                     ins=[laT_sb, hbv[dc]])
            tok = k.op("vector", lambda v, ur=ur: v.tensor_copy(
                out=u_bf.ap[ur, :], in_=B[4].ap[ur, :]), ins=[B[4]], out=u_bf)
            B[4].rtoks.append(tok)
            if c == 0:
                for dc in range(NDC):
                    k.op("vector", lambda v, dc=dc, wA=wA: v.tensor_reduce(
                        out=wbar_t.ap, in_=wfull[wA][:, D * dc:D * dc + D],
                        axis=AXX, op=ALU.add), ins=[wfull[wA]], out=wbar_t)
                    k.op("vector", lambda v, dc=dc, lA=lA: v.tensor_copy(
                        out=wbar_sb[:, 8 * lA + dc:8 * lA + dc + 1],
                        in_=wbar_t.ap), ins=[wbar_t], out=wbar_sb)
            k.mm(B[5], B[5].ap[0:1, :], lbar_sb.ap[ur, blk:blk + 1],
                 u_bf.ap[ur, :], True, False, ins=[lbar_sb, u_bf])
            for dc in range(NDC):
                k.mm(B[5], B[5].ap[0:1, :],
                     wbar_sb[:, 8 * lA + dc:8 * lA + dc + 1],
                     hbv[dc].ap, False, False,
                     ins=[wbar_sb, hbv[dc]])
            k.mm(B[5], B[5].ap[0:1, :], bbar_sb.ap[0:1, lA:lA + 1],
                 onesrowb.ap, False, True, ins=[bbar_sb, onesrowb])
            tok = k.op("vector", lambda v: v.tensor_scalar(
                out=mub_sb.ap, in0=B[5].ap[0:1, :], scalar1=1.0 / 1024.0,
                scalar2=None, op0=ALU.mult), ins=[B[5]], out=mub_sb)
            B[5].rtoks.append(tok)
            tap("u0", u_bf, c == 0 and blk == 0)
            tap("mu0", mub_sb, c == 0 and blk == 0)
            for oc in range(NDC):
                bank = B[oc % 2]
                k.mm(bank, bank.ap[:, :],
                     lbT_sb.ap[ur, D * blk + 128 * oc:D * blk + 128 * oc + 128],
                     u_bf.ap[ur, :], True, False, ins=[lbT_sb, u_bf])
                for dc in range(NDC):
                    k.mm(bank, bank.ap[:, :],
                         wfull[wA][:, D * dc + 128 * oc:D * dc + 128 * oc + 128],
                         hbv[dc].ap, False, False,
                         ins=[wfull[wA], hbv[dc]])
                k.mm(bank, bank.ap[:, :], nones1x128b.ap, mub_sb.ap,
                     False, True, ins=[nones1x128b, mub_sb])
                sl = oc % 2
                tok = k.op("scalar", lambda s, bank=bank, oc=oc, lA=lA:
                           s.activation(out=yv[oc].ap,
                                        in_=bank.ap, func=ACTF.Identity,
                                        bias=bcol_sb.ap[:, 8 * lA + oc:
                                                        8 * lA + oc + 1]),
                           ins=[bank, bcol_sb], out=yv[oc])
                bank.rtoks.append(tok)
                k.op("vector", lambda v, oc=oc, sl=sl: v.tensor_tensor(
                    out=sqy_buf[sl].ap, in0=yv[oc].ap,
                    in1=yv[oc].ap, op=ALU.mult),
                    ins=[yv[oc]], out=sqy_buf[sl])
                k.mm(B[5], B[5].ap[0:1, :], onesc128b.ap, sqy_buf[sl].ap,
                     oc == 0, oc == NDC - 1, ins=[onesc128b, sqy_buf[sl]])
            nl = lA + 3
            if nl < 6:
                load_w(nl, nl % 3)
            elif c + 1 < NCH:
                load_w(nl - 6, (nl - 6) % 3)
            tap("y0", y_sb, c == 0 and blk == 0)
            tap("y1", y_sb, c == 0 and blk == 1)
            tok = k.op("vector", lambda v: v.tensor_scalar(
                out=st_var.ap, in0=B[5].ap[0:1, :], scalar1=1.0 / 1024.0,
                scalar2=None, op0=ALU.mult), ins=[B[5]], out=st_var)
            B[5].rtoks.append(tok)
            k.op("vector", lambda v: v.tensor_scalar(
                out=st_ln.ap, in0=st_var.ap, scalar1=1e-5, scalar2=None,
                op0=ALU.add), ins=[st_var], out=st_ln)
            k.op("scalar", lambda s: s.activation(out=st_var.ap, in_=st_ln.ap,
                                                  func=ACTF.Ln),
                 ins=[st_ln], out=st_var)
            k.op("scalar", lambda s: s.activation(out=st_rs.ap, in_=st_var.ap,
                                                  func=ACTF.Exp, scale=-0.5),
                 ins=[st_var], out=st_rs)
            k.mm(B[6], B[6].ap[:, :], ones1x128f.ap, st_rs.ap, True, True,
                 ins=[ones1x128f, st_rs])
            for oc in range(NDC):
                sl = oc % 2
                tok = k.op("vector", lambda v, oc=oc, sl=sl, blk=blk:
                           v.scalar_tensor_tensor(
                               out=t4_buf[sl].ap,
                               in0=yv[oc].ap,
                               scalar=gam_sb.ap[:, 8 * blk + oc:
                                                8 * blk + oc + 1],
                               in1=B[6].ap, op0=ALU.mult, op1=ALU.mult),
                           ins=[yv[oc], gam_sb, B[6]], out=t4_buf[sl])
                B[6].rtoks.append(tok)
                k.op("vector", lambda v, sl=sl, blk=blk, oc=oc: v.tensor_scalar(
                    out=t5_buf[sl].ap, in0=t4_buf[sl].ap,
                    scalar1=bet_sb.ap[:, 8 * blk + oc:8 * blk + oc + 1],
                    scalar2=None, op0=ALU.add),
                    ins=[t4_buf[sl], bet_sb], out=t5_buf[sl])
                k.op("scalar", lambda s, oc=oc, sl=sl: s.activation(
                    out=g1v[oc].ap, in_=t5_buf[sl].ap,
                    func=ACTF.Gelu), ins=[t5_buf[sl]], out=g1v[oc])
            tap("g10", g1_bf, c == 0 and blk == 0)
            tap("g11", g1_bf, c == 0 and blk == 1)
            tap("rs0", st_rs, c == 0 and blk == 0)
            ur2 = slice(32, 64)
            for dc in range(NDC):
                k.mm(B[4], B[4].ap[ur2, :],
                     laT_sb[:, 192 * dc + 32 * lB:192 * dc + 32 * lB + 32],
                     g1v[dc].ap, dc == 0, dc == NDC - 1,
                     ins=[laT_sb, g1v[dc]])
            tok = k.op("vector", lambda v, ur2=ur2: v.tensor_copy(
                out=u_bf.ap[ur2, :], in_=B[4].ap[ur2, :]), ins=[B[4]],
                out=u_bf)
            B[4].rtoks.append(tok)
            for oc in range(NDC):
                bank = B[2 + oc % 2]
                k.mm(bank, bank.ap[:, :],
                     lbT_sb.ap[ur2, D * blk + 128 * oc:D * blk + 128 * oc + 128],
                     u_bf.ap[ur2, :], True, False, ins=[lbT_sb, u_bf])
                for dc in range(NDC):
                    k.mm(bank, bank.ap[:, :],
                         wfull[wB][:, D * dc + 128 * oc:D * dc + 128 * oc + 128],
                         g1v[dc].ap, False, dc == NDC - 1,
                         ins=[wfull[wB], g1v[dc]])
                tok = k.op("vector", lambda v, bank=bank, oc=oc, lB=lB,
                           hc=hcurv[oc], hn=hnewv[oc]: v.scalar_tensor_tensor(
                               out=hn.ap, in0=bank.ap,
                               scalar=bcol_sb.ap[:, 8 * lB + oc:
                                                 8 * lB + oc + 1],
                               in1=hc.ap,
                               op0=ALU.add, op1=ALU.add),
                           ins=[bank, bcol_sb, hcurv[oc]], out=hnewv[oc])
                bank.rtoks.append(tok)
                k.op("scalar", lambda s, oc=oc, hn=hnewv[oc]: s.copy(
                    out=hbv[oc].ap, in_=hn.ap), ins=[hnewv[oc]], out=hbv[oc])
            tap("h1", h_sb[1 - blk % 2], c == 0 and blk == 0)
            nl2 = lB + 3
            if nl2 < 6:
                load_w(nl2, nl2 % 3)
            elif c + 1 < NCH:
                load_w(nl2 - 6, (nl2 - 6) % 3)
        hfinv = hv[1]
        for oc in range(NDC):
            sl = oc % 2
            k.dma("sync", "si_x",
                  lambda e, oc=oc, cs=cs, sl=sl: e.dma_start(
                      out=s_in[sl].ap,
                      in_=s_scr[128 * oc:128 * oc + 128, cs]), out=s_in[sl])
            k.dma("sync", "si_x",
                  lambda e, oc=oc, cs=cs, sl=sl: e.dma_start(
                      out=x_in[sl].ap,
                      in_=ap["xT"][128 * oc:128 * oc + 128, cs]), out=x_in[sl])
            k.op("scalar", lambda s, sl=sl, hf=hfinv[oc]: s.activation(
                out=sg_buf[sl].ap, in_=hf.ap,
                func=ACTF.Sigmoid), ins=[hfinv[oc]], out=sg_buf[sl])
            k.op("vector", lambda v, sl=sl, hf=hfinv[oc]: v.tensor_tensor(
                out=gg_buf[sl].ap, in0=hf.ap,
                in1=sg_buf[sl].ap, op=ALU.mult),
                ins=[hfinv[oc], sg_buf[sl]], out=gg_buf[sl])
            k.op("vector", lambda v, sl=sl: v.tensor_tensor(
                out=e2_buf[sl].ap, in0=gg_buf[sl].ap, in1=s_in[sl].ap,
                op=ALU.mult), ins=[gg_buf[sl], s_in[sl]], out=e2_buf[sl])
            k.op("vector", lambda v, sl=sl: v.tensor_tensor(
                out=o_buf[sl].ap, in0=e2_buf[sl].ap, in1=x_in[sl].ap,
                op=ALU.add), ins=[e2_buf[sl], x_in[sl]], out=o_buf[sl])
            k.dma("sync", "si_out",
                  lambda e, oc=oc, cs=cs, sl=sl: e.dma_start(
                      out=outT[128 * oc:128 * oc + 128, cs],
                      in_=o_buf[sl].ap), ins=[o_buf[sl]])

    es2.close()
    final = [("si_out", k.cnt["si_out"])]
    k.finalize(final)
    k.es.close()
    return nc


def _host_constants():
    c = {}
    c["ident"] = np.eye(128, dtype=BF)
    p = np.arange(128)
    c["diagmask"] = (p[:, None] <= p[None, :]).astype(BF)
    c["selhalf"] = (p[:, None] // 64 == np.arange(2)[None, :]).astype(BF)
    c["selpair_f"] = (np.arange(2)[:, None] == p[None, :] // 64).astype(np.float32)
    c["selrq_f"] = (np.arange(2)[:, None] == (p[None, :] % 64) // 32).astype(np.float32)
    c["ones_1x128_b"] = np.ones((1, 128), BF)
    c["nones_1x128_b"] = -np.ones((1, 128), BF)
    c["ones_1x128_f"] = np.ones((1, 128), np.float32)
    c["ones_c128_b"] = np.ones((128, 1), BF)
    c["ones_c128_f"] = np.ones((128, 1), np.float32)
    c["ones_row_b"] = np.ones((1, C), BF)
    oc3 = np.zeros((128, 3), BF)
    oc3[:, 0] = 1.0
    oc3[:, 2] = 1.0
    c["onescol3"] = oc3
    return c


def _fm_cols(v):
    n = v.shape[0]
    out = np.zeros((128, 8 * n), np.float32)
    for i in range(n):
        out[:, 8 * i:8 * i + 8] = v[i].reshape(8, 128).T
    return out


def prep_in_maps(L0_attn_x, L0_attn_Wqkv, L0_attn_bqkv, L0_attn_Wout,
                 L0_attn_bout, L0_attn_omega, L0_attn_quad_nodes,
                 L0_attn_quad_weights, L0_ffn_qweights, L0_ffn_scales,
                 L0_ffn_biases, L0_ffn_lora_a, L0_ffn_lora_b,
                 L0_ffn_ln_gamma, L0_ffn_ln_beta):
    x = np.ascontiguousarray(np.asarray(L0_attn_x, np.float32))
    Wqkv = np.ascontiguousarray(np.asarray(L0_attn_Wqkv, np.float32))
    bqkv = np.asarray(L0_attn_bqkv, np.float32)
    Wout = np.ascontiguousarray(np.asarray(L0_attn_Wout, np.float32))
    bout = np.asarray(L0_attn_bout, np.float32)
    omega = np.asarray(L0_attn_omega, np.float32)
    qnodes = np.asarray(L0_attn_quad_nodes, np.float32)
    qweights = np.asarray(L0_attn_quad_weights, np.float32)
    qint = np.ascontiguousarray(np.asarray(L0_ffn_qweights, np.int32))
    scales = np.asarray(L0_ffn_scales, np.float32)
    biases = np.asarray(L0_ffn_biases, np.float32)
    lora_a = np.asarray(L0_ffn_lora_a, np.float32)
    lora_b = np.asarray(L0_ffn_lora_b, np.float32)
    ln_g = np.asarray(L0_ffn_ln_gamma, np.float32)
    ln_b = np.asarray(L0_ffn_ln_beta, np.float32)

    omgblk = np.zeros((128, 8 * 128), np.float32)
    for i in range(8):
        for par in range(2):
            h = 2 * i + par
            O = np.transpose(omega[:, h], (1, 0, 2)).reshape(64, 64)
            omgblk[64 * par:64 * par + 64,
                   128 * i + 64 * par:128 * i + 64 * par + 64] = O
    qT = np.ascontiguousarray(qint.transpose(0, 2, 1)).reshape(6 * D, D)
    scT = np.ascontiguousarray(
        np.repeat(scales, 16, axis=2).transpose(0, 2, 1)).reshape(6 * D, D)
    laT = np.ascontiguousarray(lora_a.transpose(2, 0, 1)).reshape(D, 192)
    lbT = np.zeros((64, 3 * D), np.float32)
    for b_ in range(3):
        lbT[0:32, D * b_:D * (b_ + 1)] = lora_b[2 * b_].T
        lbT[32:64, D * b_:D * (b_ + 1)] = lora_b[2 * b_ + 1].T

    shared = {
        "wqkv": Wqkv, "wout": Wout,
        "bq_row": np.ascontiguousarray(bqkv[None, 0:D]),
        "bk_row": np.ascontiguousarray(bqkv[None, D:2 * D]),
        "bv_row": np.ascontiguousarray(bqkv[None, 2 * D:3 * D]),
        "bout_fm": np.ascontiguousarray(bout.reshape(8, 128).T),
        "omgblk": omgblk,
        "qn2": np.ascontiguousarray(qnodes.reshape(2, 1)),
        "qw2": np.ascontiguousarray(qweights.reshape(2, 1)),
        "qT": qT, "scT": scT,
        "bcol_fm": _fm_cols(biases), "gam_fm": _fm_cols(ln_g),
        "bet_fm": _fm_cols(ln_b),
        "laT": laT, "lbT": lbT,
    }
    shared.update(_host_constants())

    in_maps = []
    for b_ in range(8):
        m = dict(shared)
        m["xT"] = np.ascontiguousarray(x[b_].T)
        in_maps.append(m)
    return in_maps


def kernel(**inputs):
    in_maps = prep_in_maps(**inputs)
    nc = build_program()
    res = run_bass_kernel_spmd(nc, in_maps, core_ids=list(range(8)))
    out = np.stack([np.ascontiguousarray(np.asarray(r["outT"]).T)
                    for r in res.results])
    return np.ascontiguousarray(out)


# version 12
# speedup vs baseline: 1.0434x; 1.0206x over previous
import contextlib
import numpy as np
import ml_dtypes

import concourse.bass as bass
import concourse.mybir as mybir
from concourse.bass_utils import run_bass_kernel_spmd

F32 = mybir.dt.float32
BF16 = mybir.dt.bfloat16
I32 = mybir.dt.int32
ALU = mybir.AluOpType
ACTF = mybir.ActivationFunctionType
BF = ml_dtypes.bfloat16
AXX = mybir.AxisListType.X

D = 1024
T = 4096
C = 512
NCH = T // C
NDC = D // 128
ATM_OFF = (0, 512, 896, 1152)
ATM_W = (512, 384, 256, 128)

ENGINES = ("sync", "gpsimd", "tensor", "vector", "scalar")
ESEM = {"tensor": "e_t", "vector": "e_v", "scalar": "e_s", "gpsimd": "e_g",
        "sync": "e_y"}


class Buf:
    __slots__ = ("ap", "wtok", "rtoks")

    def __init__(self, ap):
        self.ap = ap
        self.wtok = None
        self.rtoks = []

    def __getitem__(self, key):
        return self.ap[key]


class Bank(Buf):
    __slots__ = ("name",)

    def __init__(self, ap, name):
        super().__init__(ap)
        self.name = name


class K:
    def __init__(self, nc):
        self.nc = nc
        self.es = contextlib.ExitStack()
        self.ops = {e: [] for e in ENGINES}
        self.cnt = {}
        for s in ("si_x", "si_w", "si_m", "si_sc", "si_out", "sg", "sg_su",
                  "sg_tap", "e_t", "e_v", "e_s", "e_g", "e_y"):
            self.cnt[s] = 0
        self._win = []

    def sb(self, name, shape, dtype, es=None):
        h = (es or self.es).enter_context(
            self.nc.sbuf_tensor(name, list(shape), dtype))
        return Buf(h[:])

    @staticmethod
    def views(buf, n, w):
        return [Buf(buf.ap[:, w * i:w * (i + 1)]) for i in range(n)]

    def bank(self, name, dtype=F32, cols=None):
        if cols is None:
            cols = 512 if dtype == F32 else 1024
        h = self.es.enter_context(self.nc.psum_tensor(name, [128, cols], dtype))
        return Bank(h[:], name)

    paranoid = 0

    def _emit(self, eng, fn, waits, inc, d):
        if self.paranoid == 1:
            waits = list(waits) + [(s, v) for s, v in self.cnt.items() if v > 0]
        elif self.paranoid == 2:
            waits = list(waits) + [(s, v) for s, v in self.cnt.items()
                                   if v > 0 and s.startswith("e_")]
        elif self.paranoid == 3:
            waits = list(waits) + [(s, v) for s, v in self.cnt.items()
                                   if v > 0 and not s.startswith("e_")]
        self.cnt[inc] += d
        self.ops[eng].append((fn, tuple(waits), (inc, d)))
        return (inc, self.cnt[inc])

    @staticmethod
    def _waits(ins):
        ws = []
        for o in ins:
            if o is None:
                continue
            if isinstance(o, tuple):
                ws.append(o)
            elif o.wtok is not None:
                ws.append(o.wtok)
        return ws

    def _do(self, eng, fn, ins, out, sem, d, waw=True):
        waits = self._waits(ins)
        if out is not None:
            if waw and out.wtok is not None:
                waits.append(out.wtok)
            waits += out.rtoks
        waits = [(s, self.cnt[s]) if not s.startswith("e_") else (s, v)
                 for (s, v) in waits]
        tok = self._emit(eng, fn, waits, sem, d)
        for o in ins:
            if o is not None and not isinstance(o, tuple):
                o.rtoks.append(tok)
        if out is not None:
            out.wtok = tok
            out.rtoks = []
        return tok

    def op(self, eng, fn, ins=(), out=None):
        return self._do(eng, fn, ins, out, ESEM[eng], 1)

    def dma(self, eng, sem, fn, ins=(), out=None, waw=True):
        return self._do(eng, fn, ins, out, sem, 16, waw=waw)

    def mm(self, bank, out_ap, lhsT, rhs, start, stop, ins=(),
           is_transpose=False):
        base = lhsT.base_partition()
        lo, hi = base // 32, (base + lhsT.shape[0] + 31) // 32
        if hi - lo >= 4:
            self._win = []
        else:
            for (bn, l2, h2) in self._win:
                if bn == bank.name and (h2 <= lo or hi <= l2):
                    raise RuntimeError(
                        f"strip conflict bank={bank.name} ({l2},{h2})vs({lo},{hi})")
            self._win = [w for w in self._win if w[0] != bank.name] + \
                [(bank.name, lo, hi)]
        ins = list(ins)
        if start:
            ins += bank.rtoks
            bank.rtoks = []
        kw = {"is_transpose": True} if is_transpose else {}
        tok = self.op("tensor",
                      lambda t, o=out_ap, a=lhsT, b=rhs, s1=start, s2=stop, kw=kw:
                      t.matmul(o, a, b, start=s1, stop=s2, **kw),
                      ins=ins)
        bank.wtok = tok
        return tok

    def finalize(self, final_waits):
        nc = self.nc
        referenced = {}
        for ename in ENGINES:
            for _, waits, _ in self.ops[ename]:
                for (s, v) in waits:
                    referenced.setdefault(s, set()).add(v)
        for (s, v) in final_waits:
            referenced.setdefault(s, set()).add(v)
        remap = {}
        newcnt = {}
        for s in self.cnt:
            if not s.startswith("e_"):
                continue
            keep = sorted(referenced.get(s, ()))
            remap[s] = {v: i + 1 for i, v in enumerate(keep)}
            newcnt[s] = len(keep)

        def map_wait(s, v):
            if s in remap:
                return (s, remap[s][v])
            return (s, v)

        sems = {}
        with contextlib.ExitStack() as st:
            for name in self.cnt:
                sems[name] = st.enter_context(nc.semaphore(name))
            block = st.enter_context(nc.Block())

            def mk(ename):
                def body(eng):
                    hw = {}
                    run = {s: 0 for s in self.cnt}
                    for fn, waits, (isem, d) in self.ops[ename]:
                        for (s0, v0) in waits:
                            s, v = map_wait(s0, v0)
                            if hw.get(s, 0) < v:
                                eng.wait_ge(sems[s], v)
                                hw[s] = v
                        inst = fn(eng)
                        if isem in remap:
                            run[isem] += d
                            nv = remap[isem].get(run[isem])
                            if nv is not None:
                                inst.then_inc(sems[isem], nv - remap[isem].get(
                                    max((x for x in remap[isem] if x < run[isem]),
                                        default=0), 0) if False else 1)
                        else:
                            inst.then_inc(sems[isem], d)
                    if ename == "sync":
                        for (s0, v0) in final_waits:
                            s, v = map_wait(s0, v0)
                            eng.wait_ge(sems[s], v)
                return body

            for ename in ENGINES:
                getattr(block, ename)(mk(ename))


def build_program(taps=(), paranoid=0):
    nc = bass.Bass("TRN2", target_bir_lowering=False)
    k = K(nc)
    k.paranoid = paranoid
    ap = {}

    def inp(name, shape, dtype):
        ap[name] = nc.dram_tensor(name, list(shape), dtype,
                                  kind="ExternalInput").ap()

    inp("xT", [D, T], F32)
    inp("wqkv", [D, 3 * D], F32)
    inp("wout", [D, D], F32)
    inp("bq_row", [1, D], F32)
    inp("bk_row", [1, D], F32)
    inp("bv_row", [1, D], F32)
    inp("bout_fm", [128, NDC], F32)
    inp("omgblk", [128, 8 * 128], F32)
    inp("qn2", [2, 1], F32)
    inp("qw2", [2, 1], F32)
    inp("qT", [6 * D, D], I32)
    inp("scT", [6 * D, D], F32)
    inp("bcol_fm", [128, 48], F32)
    inp("gam_fm", [128, 24], F32)
    inp("bet_fm", [128, 24], F32)
    inp("laT", [D, 192], F32)
    inp("lbT", [64, 3 * D], F32)
    inp("ident", [128, 128], BF16)
    inp("diagmask", [128, 128], BF16)
    inp("selhalf", [128, 2], BF16)
    inp("selpair_f", [2, 128], F32)
    inp("selrq_f", [2, 128], F32)
    inp("ones_1x128_b", [1, 128], BF16)
    inp("nones_1x128_b", [1, 128], BF16)
    inp("ones_1x128_f", [1, 128], F32)
    inp("ones_c128_b", [128, 1], BF16)
    inp("ones_c128_f", [128, 1], F32)
    inp("ones_row_b", [1, C], BF16)
    inp("onescol3", [128, 3], BF16)

    xn_scr = nc.dram_tensor("xn_scr", [D, T], F32, kind="Internal").ap()
    s_scr = nc.dram_tensor("s_scr", [D, T], F32, kind="Internal").ap()
    w_scr = nc.dram_tensor("w_scr", [6 * D, D], BF16, kind="Internal").ap()
    outT = nc.dram_tensor("outT", [D, T], F32, kind="ExternalOutput").ap()
    tap_ap = {}
    for tname, tshape in [("qf0", [128, NDC * C]), ("kf0", [128, NDC * C]),
                          ("v0", [128, 4 * 1040]), ("attn0", [128, NDC * C]),
                          ("y0", [128, NDC * C]), ("g10", [128, NDC * C]),
                          ("h1", [128, NDC * C]), ("u0", [64, C]),
                          ("mu0", [1, C]), ("rs0", [1, C]),
                          ("h2", [128, NDC * C]), ("h3", [128, NDC * C]),
                          ("gg0", [128, C]), ("y1", [128, NDC * C]),
                          ("g11", [128, NDC * C])]:
        if tname in taps:
            tap_ap[tname] = nc.dram_tensor(tname, tshape, F32,
                                           kind="ExternalOutput").ap()

    def tap(tname, buf, cond=True):
        if tname in tap_ap and cond:
            k.dma("gpsimd", "sg_tap",
                  lambda e, d=tap_ap[tname], s=buf.ap: e.dma_start(out=d, in_=s),
                  ins=[buf])

    B = [k.bank(f"pb{i}") for i in range(7)]
    PT = k.bank("pbt", BF16)
    es1 = contextlib.ExitStack()

    ident_sb = k.sb("ident_sb", [128, 128], BF16)
    selhalf_sb = k.sb("selhalf_sb", [128, 2], BF16)
    selpair_sb = k.sb("selpair_sb", [2, 128], F32)
    selrq_sb = k.sb("selrq_sb", [2, 128], F32)
    ones1x128b = k.sb("ones1x128b", [1, 128], BF16)
    nones1x128b = k.sb("nones1x128b", [1, 128], BF16)
    ones1x128f = k.sb("ones1x128f", [1, 128], F32)
    onesc128b = k.sb("onesc128b", [128, 1], BF16)
    onesc128f = k.sb("onesc128f", [128, 1], F32)
    onesrowb = k.sb("onesrowb", [1, C], BF16)
    onescol3 = k.sb("onescol3_sb", [128, 3], BF16)
    bout_sb = k.sb("bout_sb", [128, NDC], F32)
    bcol_sb = k.sb("bcol_sb", [128, 48], F32)
    gam_sb = k.sb("gam_sb", [128, 24], F32)
    bet_sb = k.sb("bet_sb", [128, 24], F32)
    laT_sb = k.sb("laT_sb", [128, NDC * 192], BF16)
    lbT_sb = k.sb("lbT_sb", [64, 3 * D], BF16)
    wbar_sb = k.sb("wbar_sb", [128, 48], BF16)
    lbar_f = k.sb("lbar_f", [64, 3], F32)
    lbar_sb = k.sb("lbar_sb", [64, 3], BF16)
    bsum_col = k.sb("bsum_col", [128, 6], F32)
    bbar_sb = k.sb("bbar_sb", [1, 6], BF16)
    hi_col = k.sb("hi_col", [128, 1], F32)
    lo_col = k.sb("lo_col", [128, 1], F32)
    lncs_col = k.sb("lncs_col", [128, 1], F32)
    scol_sb = k.sb("scol_sb", [128, 1], F32)
    qn_sb = k.sb("qn_sb", [2, 1], F32)
    qw_sb = k.sb("qw_sb", [2, 1], F32)
    su = [k.sb(f"su{i}", [2, 1], F32) for i in range(4)]
    st_var = k.sb("st_var", [1, C], F32)
    st_ln = k.sb("st_ln", [1, C], F32)
    st_rs = k.sb("st_rs", [1, C], F32)

    wqkv_sb = k.sb("wqkv_sb", [128, NDC * 3 * D], BF16, es=es1)
    wout_sb = k.sb("wout_sb", [128, NDC * D], BF16, es=es1)
    omg_sb = k.sb("omg_sb", [128, 8 * 128], BF16, es=es1)
    dmask_sb = k.sb("dmask_sb", [128, 128], BF16, es=es1)
    bqkv_row = k.sb("bqkv_row", [1, 2 * D], BF16, es=es1)
    bv_row_sb = k.sb("bv_row_sb", [1, D], BF16, es=es1)
    s2s_row = k.sb("s2s_row", [1, 128], F32, es=es1)
    s2s_2 = k.sb("s2s_2", [2, 128], F32, es=es1)
    sel2s = k.sb("sel2s", [2, 128], F32, es=es1)
    S_sb = [k.sb(f"S_sb{p}", [128, 8 * 65], F32, es=es1) for p in range(2)]
    SZ_bf = k.sb("SZ_bf", [128, 65], BF16, es=es1)
    zq_sb = k.sb("zq_sb", [128, 4], BF16, es=es1)
    def ldc(dst_buf, src_ap, dst_ap=None):
        da = dst_ap if dst_ap is not None else dst_buf.ap
        return k.dma("sync", "si_m",
                     lambda e, d=da, s=src_ap: e.dma_start(out=d, in_=s),
                     out=dst_buf)

    def ldg(dst_buf, src_ap, dst_ap=None, sem="sg_su"):
        da = dst_ap if dst_ap is not None else dst_buf.ap
        return k.dma("gpsimd", sem,
                     lambda e, d=da, s=src_ap: e.dma_start(out=d, in_=s),
                     out=dst_buf)

    for b_, nm in [(ident_sb, "ident"), (dmask_sb, "diagmask"),
                   (selhalf_sb, "selhalf"), (selpair_sb, "selpair_f"),
                   (selrq_sb, "selrq_f"), (ones1x128b, "ones_1x128_b"),
                   (nones1x128b, "nones_1x128_b"), (ones1x128f, "ones_1x128_f"),
                   (onesc128b, "ones_c128_b"), (onesc128f, "ones_c128_f"),
                   (onesrowb, "ones_row_b"), (onescol3, "onescol3"),
                   (bout_sb, "bout_fm"), (bcol_sb, "bcol_fm"),
                   (gam_sb, "gam_fm"), (bet_sb, "bet_fm"),
                   (qn_sb, "qn2"), (qw_sb, "qw2")]:
        ldc(b_, ap[nm])
    ldg(omg_sb, ap["omgblk"])
    ldg(bqkv_row, ap["bq_row"], dst_ap=bqkv_row[0:1, 0:D])
    ldg(bqkv_row, ap["bk_row"], dst_ap=bqkv_row[0:1, D:2 * D])
    ldg(bv_row_sb, ap["bv_row"])
    for dc in range(NDC):
        r = slice(128 * dc, 128 * dc + 128)
        ldg(wqkv_sb, ap["wqkv"][r, :],
            dst_ap=wqkv_sb[:, 3 * D * dc:3 * D * (dc + 1)])
        ldg(wout_sb, ap["wout"][r, :], dst_ap=wout_sb[:, D * dc:D * (dc + 1)])
        ldg(laT_sb, ap["laT"][r, :], dst_ap=laT_sb[:, 192 * dc:192 * (dc + 1)])
    ldg(lbT_sb, ap["lbT"])

    k.op("vector", lambda v: v.tensor_scalar(
        out=su[0][0:2, :], in0=qn_sb[0:2, :], scalar1=0.0, scalar2=2.0,
        op0=ALU.max, op1=ALU.mult), ins=[qn_sb], out=su[0])
    k.op("scalar", lambda s: s.activation(out=su[1][0:2, :], in_=su[0][0:2, :],
                                          func=ACTF.Ln), ins=[su[0]], out=su[1])
    k.op("scalar", lambda s: s.activation(out=su[2][0:2, :], in_=su[1][0:2, :],
                                          func=ACTF.Exp, scale=0.5),
         ins=[su[1]], out=su[2])
    k.mm(B[0], B[0].ap[0:1, 0:128], su[2].ap[0:2, 0:1], selrq_sb.ap, True, True,
         ins=[su[2], selrq_sb])
    tok = k.op("vector", lambda v: v.tensor_copy(out=s2s_row.ap,
                                                 in_=B[0].ap[0:1, 0:128]),
               ins=[B[0]], out=s2s_row)
    B[0].rtoks.append(tok)
    k.mm(B[1], B[1].ap[0:2, 0:128], ones1x128f.ap[0:1, 0:2], s2s_row.ap,
         True, True, ins=[ones1x128f, s2s_row])
    tok = k.op("vector", lambda v: v.tensor_copy(out=s2s_2.ap,
                                                 in_=B[1].ap[0:2, 0:128]),
               ins=[B[1]], out=s2s_2)
    B[1].rtoks.append(tok)
    k.op("vector", lambda v: v.tensor_tensor(
        out=sel2s.ap, in0=selpair_sb.ap, in1=s2s_2.ap, op=ALU.mult),
        ins=[selpair_sb, s2s_2], out=sel2s)
    k.mm(B[2], B[2].ap[:, 0:1], selrq_sb.ap, qn_sb.ap[0:2, :], True, True,
         ins=[selrq_sb, qn_sb])
    tok = k.op("vector", lambda v: v.tensor_copy(out=scol_sb.ap,
                                                 in_=B[2].ap[:, 0:1]),
               ins=[B[2]], out=scol_sb)
    B[2].rtoks.append(tok)
    k.op("vector", lambda v: v.tensor_scalar(
        out=hi_col.ap, in0=scol_sb.ap, scalar1=20.0, scalar2=None, op0=ALU.add),
        ins=[scol_sb], out=hi_col)
    k.op("vector", lambda v: v.tensor_scalar(
        out=lo_col.ap, in0=scol_sb.ap, scalar1=-20.0, scalar2=None, op0=ALU.add),
        ins=[scol_sb], out=lo_col)
    k.op("vector", lambda v: v.tensor_scalar(
        out=su[3][0:2, :], in0=qw_sb[0:2, :], scalar1=0.0, scalar2=None,
        op0=ALU.max), ins=[qw_sb], out=su[3])
    k.op("scalar", lambda s: s.activation(out=su[0][0:2, :], in_=su[3][0:2, :],
                                          func=ACTF.Ln), ins=[su[3]], out=su[0])
    k.op("vector", lambda v: v.tensor_scalar(
        out=su[1][0:2, :], in0=su[0][0:2, :], scalar1=0.5,
        scalar2=-0.5 * float(np.log(32.0)), op0=ALU.mult, op1=ALU.add),
        ins=[su[0]], out=su[1])
    k.mm(B[3], B[3].ap[:, 0:1], selrq_sb.ap, su[1].ap[0:2, :], True, True,
         ins=[selrq_sb, su[1]])
    tok = k.op("vector", lambda v: v.scalar_tensor_tensor(
        out=lncs_col.ap, in0=B[3].ap[:, 0:1], scalar=1.0, in1=scol_sb.ap,
        op0=ALU.bypass, op1=ALU.subtract), ins=[B[3], scol_sb], out=lncs_col)
    B[3].rtoks.append(tok)
    for l in range(6):
        k.op("vector", lambda v, l=l: v.tensor_reduce(
            out=bsum_col[:, l:l + 1], in_=bcol_sb[:, 8 * l:8 * l + 8],
            axis=AXX, op=ALU.add), ins=[bcol_sb], out=bsum_col)
    k.mm(B[4], B[4].ap[0:1, 0:6], onesc128f.ap, bsum_col.ap, True, True,
         ins=[onesc128f, bsum_col])
    tok = k.op("vector", lambda v: v.tensor_copy(out=bbar_sb.ap,
                                                 in_=B[4].ap[0:1, 0:6]),
               ins=[B[4]], out=bbar_sb)
    B[4].rtoks.append(tok)
    for b_ in range(3):
        k.op("vector", lambda v, b_=b_: v.tensor_reduce(
            out=lbar_f[0:64, b_:b_ + 1], in_=lbT_sb[0:64, D * b_:D * (b_ + 1)],
            axis=AXX, op=ALU.add), ins=[lbT_sb], out=lbar_f)
    k.op("vector", lambda v: v.tensor_copy(out=lbar_sb.ap, in_=lbar_f.ap),
         ins=[lbar_f], out=lbar_sb)

    x_sb = k.sb("x_sb", [128, NDC * C], F32, es=es1)
    xb_buf = [k.sb("xb0", [128, C], BF16, es=es1)] * 2
    sqx_buf = [k.sb("sqx0", [128, C], BF16, es=es1)] * 2
    t1_buf = [k.sb("t1_0", [128, C], F32, es=es1)] * 2
    xnf_buf = [k.sb(f"xnf{i}", [128, C], F32, es=es1) for i in range(2)]
    xn_bf = k.sb("xn_bf", [128, NDC * C], BF16, es=es1)
    xnv = K.views(xn_bf, NDC, C)
    st_mu = k.sb("st_mu", [1, C], F32, es=es1)
    st_msq = k.sb("st_msq", [1, C], F32, es=es1)
    st_nmu = k.sb("st_nmu", [1, C], BF16, es=es1)
    qb_buf = [k.sb(f"qb{i}", [128, C], BF16, es=es1) for i in range(4)]
    sq_buf = [k.sb(f"sq{i}", [128, C], BF16, es=es1) for i in range(2)]
    s2_sb = [k.sb(f"s2_{i}", [2, C], F32, es=es1) for i in range(4)]
    rln_sb = [k.sb("rln0", [2, C], F32, es=es1)]
    rex_sb = [k.sb("rex0", [2, C], F32, es=es1)]
    rmin_sb = rln_sb
    rbs_sb = [k.sb(f"rbs{i}", [128, C], F32, es=es1) for i in range(2)]
    y_buf = [k.sb("yb0", [128, C], F32, es=es1)] * 2
    c2_buf = [k.sb("c2_0", [128, C], F32, es=es1)] * 2
    qf_sb = k.sb("qf_sb", [128, NDC * C], BF16, es=es1)
    kf_sb = k.sb("kf_sb", [128, NDC * C], BF16, es=es1)
    qfv = K.views(qf_sb, NDC, C)
    kfv = K.views(kf_sb, NDC, C)
    attnv = qfv
    v_sb = k.sb("v_sb", [128, 4 * 1040], BF16, es=es1)
    vv = K.views(v_sb, 4, 1040)
    kfT_sb = k.sb("kfT_sb", [128, 512], BF16, es=es1)
    atm_sb = k.sb("atm_sb", [128, 2 * 1280], BF16, es=es1)
    den_sb = k.sb("den_sb", [2, C], F32, es=es1)
    dln_sb = k.sb("dln_sb", [2, C], F32, es=es1)
    recip_sb = den_sb
    ctx_sb = k.sb("ctx_sb", [128, C], F32, es=es1)
    sout_buf = [k.sb(f"sout{i}", [128, C], F32, es=es1) for i in range(2)]
    dq_qi = [k.sb("dqqi0", [128, 512], I32, es=es1)]
    dq_sc = [k.sb("dqsc0", [128, 512], F32, es=es1)]
    dq_t = [k.sb("dqt0", [128, 512], F32, es=es1)]
    dq_w = [k.sb("dqw0", [128, 512], BF16, es=es1)]

    for tt in range(4):
        v65 = vv[tt].ap.rearrange("p (a b) -> p a b", b=65)
        k.op("gpsimd", lambda g, v65=v65: g.memset(v65[:, :, 64:65], 1.0),
             out=vv[tt])
    k.op("gpsimd", lambda g: g.memset(zq_sb[:, 1:3], 0.0), out=zq_sb)
    k.op("gpsimd", lambda g: g.memset(S_sb[0].ap, 0.0), out=S_sb[0])

    def emit_dequant(lo, hi):
        for idx in range(lo, hi):
            half = idx % 2
            l, dc = divmod(idx // 2, NDC)
            sl = 0
            r = slice(D * l + 128 * dc, D * l + 128 * dc + 128)
            cseg = slice(512 * half, 512 * half + 512)
            ldg(dq_qi[sl], ap["qT"][r, cseg], sem="sg")
            ldg(dq_sc[sl], ap["scT"][r, cseg], sem="sg")
            k.op("gpsimd", lambda g, sl=sl: g.tensor_scalar(
                out=dq_t[sl].ap, in0=dq_qi[sl].ap, scalar1=1.0 / 7.5,
                scalar2=-1.0, op0=ALU.mult, op1=ALU.add),
                ins=[dq_qi[sl]], out=dq_t[sl])
            k.op("gpsimd", lambda g, sl=sl: g.tensor_tensor(
                out=dq_w[sl].ap, in0=dq_t[sl].ap, in1=dq_sc[sl].ap,
                op=ALU.mult), ins=[dq_t[sl], dq_sc[sl]], out=dq_w[sl])
            k.dma("gpsimd", "sg",
                  lambda e, sl=sl, r=r, cseg=cseg: e.dma_start(
                      out=w_scr[r, cseg], in_=dq_w[sl].ap), ins=[dq_w[sl]])


    def feat_s1(oc, bank, which="q"):
        qb = qb_buf[2 * (oc % 2) + (0 if which == "q" else 1)]
        sl = 2 * (oc % 2) + (0 if which == "q" else 1)
        tok = k.op("scalar", lambda s, qb=qb, bank=bank: s.copy(
            out=qb.ap, in_=bank.ap), ins=[bank], out=qb)
        bank.rtoks.append(tok)
        k.op("vector", lambda v, qb=qb, sl=sl: v.tensor_tensor(
            out=sq_buf[sl % 2].ap, in0=qb.ap, in1=qb.ap, op=ALU.mult),
            ins=[qb], out=sq_buf[sl % 2])
        k.mm(B[4], B[4].ap[0:2, :], selhalf_sb.ap, sq_buf[sl % 2].ap,
             True, True, ins=[selhalf_sb, sq_buf[sl % 2]])
        tok = k.op("vector", lambda v, sl=sl: v.tensor_copy(
            out=s2_sb[sl].ap, in_=B[4].ap[0:2, :]), ins=[B[4]], out=s2_sb[sl])
        B[4].rtoks.append(tok)
        return qb

    def feat_s2(oc, bank, qb, dstbuf, which="q"):
        sl = 2 * (oc % 2) + (0 if which == "q" else 1)
        k.op("scalar", lambda s, sl=sl: s.activation(
            out=rln_sb[0].ap, in_=s2_sb[sl].ap, func=ACTF.Ln),
            ins=[s2_sb[sl]], out=rln_sb[0])
        k.op("scalar", lambda s: s.activation(
            out=rex_sb[0].ap, in_=rln_sb[0].ap, func=ACTF.Exp, scale=-0.5),
            ins=[rln_sb[0]], out=rex_sb[0])
        k.op("vector", lambda v: v.tensor_scalar(
            out=rmin_sb[0].ap, in0=rex_sb[0].ap, scalar1=1e12,
            scalar2=None, op0=ALU.min), ins=[rex_sb[0]], out=rmin_sb[0])
        k.mm(B[5], B[5].ap[:, :], sel2s.ap, rmin_sb[0].ap, True, True,
             ins=[sel2s, rmin_sb[0]])
        tok = k.op("vector", lambda v: v.tensor_copy(
            out=rbs_sb[0].ap, in_=B[5].ap), ins=[B[5]], out=rbs_sb[0])
        B[5].rtoks.append(tok)
        k.mm(B[6], B[6].ap[:, :], omg_sb[:, 128 * oc:128 * oc + 128],
             qb.ap, True, True, ins=[omg_sb, qb])
        tok = k.op("vector", lambda v: v.scalar_tensor_tensor(
            out=y_buf[0].ap, in0=rbs_sb[0].ap, scalar=1.0, in1=B[6].ap,
            op0=ALU.bypass, op1=ALU.mult), ins=[rbs_sb[0], B[6]],
            out=y_buf[0])
        B[6].rtoks.append(tok)
        k.op("vector", lambda v: v.tensor_scalar(
            out=c2_buf[0].ap, in0=y_buf[0].ap, scalar1=hi_col.ap,
            scalar2=lo_col.ap, op0=ALU.min, op1=ALU.max),
            ins=[y_buf[0], hi_col, lo_col], out=c2_buf[0])
        k.op("scalar", lambda s, dstbuf=dstbuf: s.activation(
            out=dstbuf.ap, in_=c2_buf[0].ap,
            func=ACTF.Exp, bias=lncs_col.ap),
            ins=[c2_buf[0], lncs_col], out=dstbuf)

    for c in range(NCH):
        cs = slice(C * c, C * c + C)
        pcur, pnew = c % 2, 1 - (c % 2)
        Scur, Snew = S_sb[pcur], S_sb[pnew]
        for dc in range(NDC):
            k.dma("sync", "si_x",
                  lambda e, dc=dc, cs=cs: e.dma_start(
                      out=x_sb[:, C * dc:C * dc + C],
                      in_=ap["xT"][128 * dc:128 * dc + 128, cs]),
                  out=x_sb, waw=False)
        for dc in range(NDC):
            sl = dc % 2
            xd = x_sb[:, C * dc:C * dc + C]
            k.op("scalar", lambda s, xd=xd, sl=sl: s.copy(
                out=xb_buf[sl].ap, in_=xd), ins=[x_sb], out=xb_buf[sl])
            k.op("vector", lambda v, xd=xd, sl=sl: v.tensor_tensor(
                out=sqx_buf[sl].ap, in0=xd, in1=xd, op=ALU.mult),
                ins=[x_sb], out=sqx_buf[sl])
            k.mm(B[5], B[5].ap[0:1, :], onesc128b.ap, xb_buf[sl].ap,
                 dc == 0, dc == NDC - 1, ins=[onesc128b, xb_buf[sl]])
            k.mm(B[6], B[6].ap[0:1, :], onesc128b.ap, sqx_buf[sl].ap,
                 dc == 0, dc == NDC - 1, ins=[onesc128b, sqx_buf[sl]])
        tok = k.op("vector", lambda v: v.tensor_scalar(
            out=st_mu.ap, in0=B[5].ap[0:1, :], scalar1=1.0 / 1024.0,
            scalar2=None, op0=ALU.mult), ins=[B[5]], out=st_mu)
        B[5].rtoks.append(tok)
        k.op("vector", lambda v: v.tensor_tensor(
            out=st_msq.ap, in0=st_mu.ap, in1=st_mu.ap, op=ALU.mult),
            ins=[st_mu], out=st_msq)
        tok = k.op("vector", lambda v: v.scalar_tensor_tensor(
            out=st_var.ap, in0=B[6].ap[0:1, :], scalar=1.0 / 1024.0,
            in1=st_msq.ap, op0=ALU.mult, op1=ALU.subtract),
            ins=[B[6], st_msq], out=st_var)
        B[6].rtoks.append(tok)
        k.op("vector", lambda v: v.tensor_scalar(
            out=st_ln.ap, in0=st_var.ap, scalar1=1e-5, scalar2=None,
            op0=ALU.add), ins=[st_var], out=st_ln)
        k.op("scalar", lambda s: s.activation(out=st_var.ap, in_=st_ln.ap,
                                              func=ACTF.Ln),
             ins=[st_ln], out=st_var)
        k.op("scalar", lambda s: s.activation(out=st_rs.ap, in_=st_var.ap,
                                              func=ACTF.Exp, scale=-0.5),
             ins=[st_var], out=st_rs)
        k.op("vector", lambda v: v.tensor_scalar(
            out=st_nmu.ap, in0=st_mu.ap, scalar1=-1.0, scalar2=None,
            op0=ALU.mult), ins=[st_mu], out=st_nmu)
        k.mm(B[5], B[5].ap[:, :], ones1x128b.ap, st_nmu.ap, True, True,
             ins=[ones1x128b, st_nmu])
        k.mm(B[6], B[6].ap[:, :], ones1x128f.ap, st_rs.ap, True, True,
             ins=[ones1x128f, st_rs])
        for dc in range(NDC):
            sl = dc % 2
            xd = x_sb[:, C * dc:C * dc + C]
            t_a = k.op("vector", lambda v, xd=xd, sl=sl: v.tensor_tensor(
                out=t1_buf[sl].ap, in0=xd, in1=B[5].ap, op=ALU.add),
                ins=[x_sb, B[5]], out=t1_buf[sl])
            B[5].rtoks.append(t_a)
            t_b = k.op("vector", lambda v, sl=sl: v.tensor_tensor(
                out=xnf_buf[sl].ap, in0=t1_buf[sl].ap, in1=B[6].ap,
                op=ALU.mult), ins=[t1_buf[sl], B[6]], out=xnf_buf[sl])
            B[6].rtoks.append(t_b)
            k.op("scalar", lambda s, dc=dc, sl=sl: s.copy(
                out=xnv[dc].ap, in_=xnf_buf[sl].ap),
                ins=[xnf_buf[sl]], out=xnv[dc])
            k.dma("sync", "si_sc",
                  lambda e, dc=dc, cs=cs, sl=sl: e.dma_start(
                      out=xn_scr[128 * dc:128 * dc + 128, cs],
                      in_=xnf_buf[sl].ap), ins=[xnf_buf[sl]])

        def qk_mms(oc):
            bq = B[oc % 2]
            bk = B[2 + oc % 2]
            for dc in range(NDC):
                k.mm(bq, bq.ap[:, :],
                     wqkv_sb[:, 3 * D * dc + 128 * oc:
                             3 * D * dc + 128 * oc + 128],
                     xnv[dc].ap, dc == 0, False,
                     ins=[wqkv_sb, xnv[dc]])
            k.mm(bq, bq.ap[:, :], bqkv_row[0:1, 128 * oc:128 * oc + 128],
                 onesrowb.ap, False, True, ins=[bqkv_row, onesrowb])
            for dc in range(NDC):
                k.mm(bk, bk.ap[:, :],
                     wqkv_sb[:, 3 * D * dc + D + 128 * oc:
                             3 * D * dc + D + 128 * oc + 128],
                     xnv[dc].ap, dc == 0, False,
                     ins=[wqkv_sb, xnv[dc]])
            k.mm(bk, bk.ap[:, :], bqkv_row[0:1, D + 128 * oc:D + 128 * oc + 128],
                 onesrowb.ap, False, True, ins=[bqkv_row, onesrowb])
            return bq, bk

        st1 = {}
        st2 = {}
        for oc in range(NDC + 2):
            if oc < NDC:
                bq, bk = qk_mms(oc)
                st1[oc] = (bq, bk)
            if oc - 1 in st1:
                o1 = oc - 1
                bq1, bk1 = st1.pop(o1)
                qbq = feat_s1(o1, bq1, "q")
                qbk = feat_s1(o1, bk1, "k")
                st2[o1] = (qbq, qbk, bq1, bk1)
            if oc - 2 in st2:
                o2 = oc - 2
                qbq, qbk, bq2, bk2 = st2.pop(o2)
                feat_s2(o2, bq2, qbq, qfv[o2], "q")
                feat_s2(o2, bk2, qbk, kfv[o2], "k")

        for tt in range(4):
            for half in range(2):
                bank = B[5 + half]
                for dc in range(NDC):
                    k.mm(bank, bank.ap[:, :],
                         xnv[dc].ap[:, 128 * tt:128 * tt + 128],
                         wqkv_sb[:, 3 * D * dc + 2 * D + C * half:
                                 3 * D * dc + 2 * D + C * half + C],
                         dc == 0, False, ins=[xnv[dc], wqkv_sb])
                k.mm(bank, bank.ap[:, :], ones1x128b.ap,
                     bv_row_sb[0:1, C * half:C * half + C], False, True,
                     ins=[ones1x128b, bv_row_sb])
                src = bank.ap.rearrange("p (h w) -> p h w", w=64)
                dst = vv[tt].ap[:, 520 * half:520 * half + 520]
                dst = dst.rearrange("p (h w) -> p h w", w=65)[:, :, 0:64]
                tok = k.op("vector", lambda v, d=dst, s=src: v.tensor_copy(
                    out=d, in_=s), ins=[bank], out=vv[tt])
                bank.rtoks.append(tok)

        def att_pre(i):
            k.op("vector", lambda v, i=i, Scur=Scur: v.tensor_copy(
                out=SZ_bf.ap, in_=Scur.ap[:, 65 * i:65 * i + 65]),
                ins=[Scur], out=SZ_bf)
            k.op("vector", lambda v, i=i, Scur=Scur: v.tensor_copy(
                out=zq_sb[:, 0:1], in_=Scur.ap[:, 65 * i + 64:65 * i + 65]),
                ins=[Scur], out=zq_sb)
            k.op("vector", lambda v, i=i, Scur=Scur: v.tensor_copy(
                out=zq_sb[:, 3:4], in_=Scur.ap[:, 65 * i + 64:65 * i + 65]),
                ins=[Scur], out=zq_sb)
            for j in range(4):
                k.mm(PT, PT.ap[:, 128 * j:128 * j + 128],
                     kfv[i].ap[:, 128 * j:128 * j + 128],
                     ident_sb.ap, j == 0, j == 3,
                     ins=[kfv[i], ident_sb], is_transpose=True)
            tok = k.op("vector", lambda v: v.tensor_copy(
                out=kfT_sb.ap, in_=PT.ap[:, 0:512]), ins=[PT], out=kfT_sb)
            PT.rtoks.append(tok)
            for par in range(2):
                qr = slice(64 * par, 64 * par + 64)
                for j in range(4):
                    bank = B[par + 2 * (j % 2)]
                    k.mm(bank, bank.ap[:, :],
                         kfv[i].ap[qr, 128 * j:128 * j + 128],
                         qfv[i].ap[qr, :], True, True,
                         ins=[kfv[i], qfv[i]])
                    base = 1280 * par + ATM_OFF[j]
                    t1 = k.op("vector", lambda v, bank=bank, base=base, j=j:
                              v.tensor_tensor(
                                  out=atm_sb[:, base:base + 128],
                                  in0=bank.ap[:, 128 * j:128 * j + 128],
                                  in1=dmask_sb.ap, op=ALU.mult),
                              ins=[bank, dmask_sb], out=atm_sb)
                    bank.rtoks.append(t1)
                    if j < 3:
                        w = ATM_W[j] - 128
                        t2 = k.op("scalar", lambda s, bank=bank, base=base,
                                  j=j, w=w: s.copy(
                                      out=atm_sb[:, base + 128:base + 128 + w],
                                      in_=bank.ap[:, 128 * (j + 1):C]),
                                  ins=[bank], out=atm_sb)
                        bank.rtoks.append(t2)

        def att_post(i):
            for par in range(2):
                h = 2 * i + par
                qr = slice(64 * par, 64 * par + 64)
                k.mm(B[4], B[4].ap[qr, :],
                     SZ_bf.ap[qr, 0:64], qfv[i].ap[qr, :],
                     True, False, ins=[SZ_bf, qfv[i]])
                for j in range(4):
                    k.mm(B[4], B[4].ap[qr, 128 * j:C],
                         vv[j].ap[:, 65 * h:65 * h + 64],
                         atm_sb.ap[:, 1280 * par + ATM_OFF[j]:
                                   1280 * par + ATM_OFF[j] + ATM_W[j]],
                         False, par == 1 and j == 3, ins=[vv[j], atm_sb])
            for par in range(2):
                qr = slice(64 * par, 64 * par + 64)
                zsl = (slice(0, 64), slice(0, 2)) if par == 0 else \
                    (slice(64, 128), slice(2, 4))
                k.mm(B[5], B[5].ap[0:2, :],
                     zq_sb.ap[zsl[0], zsl[1]], qfv[i].ap[qr, :],
                     par == 0, False, ins=[zq_sb, qfv[i]])
                for j in range(4):
                    k.mm(B[5], B[5].ap[0:2, 128 * j:C],
                         onescol3.ap[:, par:par + 2],
                         atm_sb.ap[:, 1280 * par + ATM_OFF[j]:
                                   1280 * par + ATM_OFF[j] + ATM_W[j]],
                         False, par == 1 and j == 3, ins=[onescol3, atm_sb])
            for par in range(2):
                h = 2 * i + par
                for j in range(4):
                    k.mm(B[6], B[6].ap[64 * par:64 * par + 64, 0:65],
                         kfT_sb.ap[:, 128 * j + 64 * par:
                                   128 * j + 64 * par + 64],
                         vv[j].ap[:, 65 * h:65 * h + 65],
                         j == 0, par == 1 and j == 3, ins=[kfT_sb, vv[j]])

        def att_eps(i):
            tok = k.op("vector", lambda v: v.tensor_scalar(
                out=den_sb.ap, in0=B[5].ap[0:2, :], scalar1=1e-6,
                scalar2=None, op0=ALU.max), ins=[B[5]], out=den_sb)
            B[5].rtoks.append(tok)
            k.op("scalar", lambda s: s.activation(out=dln_sb.ap,
                                                  in_=den_sb.ap, func=ACTF.Ln),
                 ins=[den_sb], out=dln_sb)
            k.op("scalar", lambda s: s.activation(out=recip_sb.ap,
                                                  in_=dln_sb.ap,
                                                  func=ACTF.Exp, scale=-1.0),
                 ins=[dln_sb], out=recip_sb)
            tok = k.op("vector", lambda v, i=i, Scur=Scur, Snew=Snew:
                       v.tensor_tensor(
                           out=Snew.ap[:, 65 * i:65 * i + 65],
                           in0=B[6].ap[:, 0:65],
                           in1=Scur.ap[:, 65 * i:65 * i + 65], op=ALU.add),
                       ins=[B[6], Scur], out=Snew)
            B[6].rtoks.append(tok)
            tok = k.op("vector", lambda v: v.tensor_copy(
                out=ctx_sb.ap, in_=B[4].ap), ins=[B[4]], out=ctx_sb)
            B[4].rtoks.append(tok)

        def att_fin(i):
            k.mm(B[5], B[5].ap[:, :], selpair_sb.ap, recip_sb.ap, True, True,
                 ins=[selpair_sb, recip_sb])
            tok = k.op("vector", lambda v, i=i: v.tensor_tensor(
                out=attnv[i].ap, in0=ctx_sb.ap, in1=B[5].ap,
                op=ALU.mult), ins=[ctx_sb, B[5]], out=attnv[i])
            B[5].rtoks.append(tok)

        att_pre(0)
        for i in range(NDC):
            att_post(i)
            att_eps(i)
            if i + 1 < NDC:
                att_pre(i + 1)
            att_fin(i)

        for oc in range(NDC):
            bank = B[oc % 2]
            for dc in range(NDC):
                k.mm(bank, bank.ap[:, :],
                     wout_sb[:, D * dc + 128 * oc:D * dc + 128 * oc + 128],
                     attnv[dc].ap, dc == 0, dc == NDC - 1,
                     ins=[wout_sb, attnv[dc]])
            sl = oc % 2
            tok = k.op("scalar", lambda s, bank=bank, oc=oc, sl=sl:
                       s.activation(out=sout_buf[sl].ap, in_=bank.ap,
                                    func=ACTF.Identity,
                                    bias=bout_sb.ap[:, oc:oc + 1]),
                       ins=[bank, bout_sb], out=sout_buf[sl])
            bank.rtoks.append(tok)
            k.dma("sync", "si_sc",
                  lambda e, oc=oc, cs=cs, sl=sl: e.dma_start(
                      out=s_scr[128 * oc:128 * oc + 128, cs],
                      in_=sout_buf[sl].ap), ins=[sout_buf[sl]])
        emit_dequant(12 * c, min(12 * (c + 1), 96))

    pass1_toks = [(s, k.cnt[s]) for s in ("e_t", "e_v", "e_s", "e_g",
                                          "si_x", "si_sc", "sg", "sg_su",
                                          "si_m")]
    barrier_toks = []
    for ename in ("tensor", "vector", "scalar", "gpsimd", "sync"):
        tok = k._emit(ename, lambda e: e.nop(), pass1_toks, ESEM[ename], 1)
        barrier_toks.append(tok)
    for bk in B + [PT]:
        bk.rtoks = list(barrier_toks)
        bk.wtok = None
    es1.close()

    es2 = contextlib.ExitStack()
    h_sb = [k.sb(f"h_sb{p}", [128, NDC * C], F32, es=es2) for p in range(2)]
    hv = [K.views(h_sb[p], NDC, C) for p in range(2)]
    h_bf = k.sb("h_bf", [128, NDC * C], BF16, es=es2)
    hbv = K.views(h_bf, NDC, C)
    g1_bf = k.sb("g1_bf", [128, NDC * C], BF16, es=es2)
    g1v = K.views(g1_bf, NDC, C)
    y_sb = k.sb("y_sb", [128, NDC * C], F32, es=es2)
    yv = K.views(y_sb, NDC, C)
    wfull = [k.sb(f"wfull{p}", [128, NDC * D], BF16, es=es2) for p in range(3)]
    u_bf = k.sb("u_bf", [64, C], BF16, es=es2)
    mub_sb = k.sb("mub_sb", [1, C], BF16, es=es2)
    sqy_buf = [k.sb("sqy0", [128, C], BF16, es=es2)] * 2
    t4_buf = [k.sb("t4_0", [128, C], F32, es=es2)] * 2
    t5_buf = [k.sb("t5_0", [128, C], F32, es=es2)] * 2
    sg_buf = [k.sb("sgb0", [128, C], F32, es=es2)] * 2
    gg_buf = [k.sb("ggb0", [128, C], F32, es=es2)] * 2
    e2_buf = sg_buf
    s_in = [k.sb(f"s_in{i}", [128, C], F32, es=es2) for i in range(2)]
    x_in = [k.sb(f"x_in{i}", [128, C], F32, es=es2) for i in range(2)]
    o_buf = [k.sb(f"o_buf{i}", [128, C], F32, es=es2) for i in range(2)]
    wbar_t = k.sb("wbar_t", [128, 1], F32, es=es2)

    def load_w(l, slot, extra=()):
        for dc in range(NDC):
            r = slice(D * l + 128 * dc, D * l + 128 * dc + 128)
            k.dma("sync", "si_w",
                  lambda e, r=r, dc=dc, slot=slot: e.dma_start(
                      out=wfull[slot][:, D * dc:D * dc + D], in_=w_scr[r, :]),
                  ins=list(extra), out=wfull[slot], waw=False)

    sg_done = ("sg", k.cnt["sg"])
    load_w(0, 0, extra=[sg_done])
    load_w(1, 1, extra=[sg_done])
    load_w(2, 2, extra=[sg_done])

    for c in range(NCH):
        cs = slice(C * c, C * c + C)
        for dc in range(NDC):
            k.dma("sync", "si_x",
                  lambda e, dc=dc, cs=cs: e.dma_start(
                      out=hv[0][dc].ap,
                      in_=xn_scr[128 * dc:128 * dc + 128, cs]),
                  out=hv[0][dc])
        for dc in range(NDC):
            k.op("scalar", lambda s, dc=dc: s.copy(
                out=hbv[dc].ap, in_=hv[0][dc].ap),
                ins=[hv[0][dc]], out=hbv[dc])
        for blk in range(3):
            lA, lB = 2 * blk, 2 * blk + 1
            wA, wB = lA % 3, lB % 3
            hcurv, hnewv = hv[blk % 2], hv[1 - blk % 2]
            ur = slice(0, 32)
            for dc in range(NDC):
                k.mm(B[4], B[4].ap[ur, :],
                     laT_sb[:, 192 * dc + 32 * lA:192 * dc + 32 * lA + 32],
                     hbv[dc].ap, dc == 0, dc == NDC - 1,
                     ins=[laT_sb, hbv[dc]])
            tok = k.op("vector", lambda v, ur=ur: v.tensor_copy(
                out=u_bf.ap[ur, :], in_=B[4].ap[ur, :]), ins=[B[4]], out=u_bf)
            B[4].rtoks.append(tok)
            if c == 0:
                for dc in range(NDC):
                    k.op("vector", lambda v, dc=dc, wA=wA: v.tensor_reduce(
                        out=wbar_t.ap, in_=wfull[wA][:, D * dc:D * dc + D],
                        axis=AXX, op=ALU.add), ins=[wfull[wA]], out=wbar_t)
                    k.op("vector", lambda v, dc=dc, lA=lA: v.tensor_copy(
                        out=wbar_sb[:, 8 * lA + dc:8 * lA + dc + 1],
                        in_=wbar_t.ap), ins=[wbar_t], out=wbar_sb)
            k.mm(B[5], B[5].ap[0:1, :], lbar_sb.ap[ur, blk:blk + 1],
                 u_bf.ap[ur, :], True, False, ins=[lbar_sb, u_bf])
            for dc in range(NDC):
                k.mm(B[5], B[5].ap[0:1, :],
                     wbar_sb[:, 8 * lA + dc:8 * lA + dc + 1],
                     hbv[dc].ap, False, False,
                     ins=[wbar_sb, hbv[dc]])
            k.mm(B[5], B[5].ap[0:1, :], bbar_sb.ap[0:1, lA:lA + 1],
                 onesrowb.ap, False, True, ins=[bbar_sb, onesrowb])
            tok = k.op("vector", lambda v: v.tensor_scalar(
                out=mub_sb.ap, in0=B[5].ap[0:1, :], scalar1=1.0 / 1024.0,
                scalar2=None, op0=ALU.mult), ins=[B[5]], out=mub_sb)
            B[5].rtoks.append(tok)
            tap("u0", u_bf, c == 0 and blk == 0)
            tap("mu0", mub_sb, c == 0 and blk == 0)
            for oc in range(NDC):
                bank = B[oc % 2]
                k.mm(bank, bank.ap[:, :],
                     lbT_sb.ap[ur, D * blk + 128 * oc:D * blk + 128 * oc + 128],
                     u_bf.ap[ur, :], True, False, ins=[lbT_sb, u_bf])
                for dc in range(NDC):
                    k.mm(bank, bank.ap[:, :],
                         wfull[wA][:, D * dc + 128 * oc:D * dc + 128 * oc + 128],
                         hbv[dc].ap, False, False,
                         ins=[wfull[wA], hbv[dc]])
                k.mm(bank, bank.ap[:, :], nones1x128b.ap, mub_sb.ap,
                     False, True, ins=[nones1x128b, mub_sb])
                sl = oc % 2
                tok = k.op("scalar", lambda s, bank=bank, oc=oc, lA=lA:
                           s.activation(out=yv[oc].ap,
                                        in_=bank.ap, func=ACTF.Identity,
                                        bias=bcol_sb.ap[:, 8 * lA + oc:
                                                        8 * lA + oc + 1]),
                           ins=[bank, bcol_sb], out=yv[oc])
                bank.rtoks.append(tok)
                k.op("vector", lambda v, oc=oc, sl=sl: v.tensor_tensor(
                    out=sqy_buf[sl].ap, in0=yv[oc].ap,
                    in1=yv[oc].ap, op=ALU.mult),
                    ins=[yv[oc]], out=sqy_buf[sl])
                k.mm(B[5], B[5].ap[0:1, :], onesc128b.ap, sqy_buf[sl].ap,
                     oc == 0, oc == NDC - 1, ins=[onesc128b, sqy_buf[sl]])
            nl = lA + 3
            if nl < 6:
                load_w(nl, nl % 3)
            elif c + 1 < NCH:
                load_w(nl - 6, (nl - 6) % 3)
            tap("y0", y_sb, c == 0 and blk == 0)
            tap("y1", y_sb, c == 0 and blk == 1)
            tok = k.op("vector", lambda v: v.tensor_scalar(
                out=st_var.ap, in0=B[5].ap[0:1, :], scalar1=1.0 / 1024.0,
                scalar2=None, op0=ALU.mult), ins=[B[5]], out=st_var)
            B[5].rtoks.append(tok)
            k.op("vector", lambda v: v.tensor_scalar(
                out=st_ln.ap, in0=st_var.ap, scalar1=1e-5, scalar2=None,
                op0=ALU.add), ins=[st_var], out=st_ln)
            k.op("scalar", lambda s: s.activation(out=st_var.ap, in_=st_ln.ap,
                                                  func=ACTF.Ln),
                 ins=[st_ln], out=st_var)
            k.op("scalar", lambda s: s.activation(out=st_rs.ap, in_=st_var.ap,
                                                  func=ACTF.Exp, scale=-0.5),
                 ins=[st_var], out=st_rs)
            k.mm(B[6], B[6].ap[:, :], ones1x128f.ap, st_rs.ap, True, True,
                 ins=[ones1x128f, st_rs])
            for oc in range(NDC):
                sl = oc % 2
                tok = k.op("vector", lambda v, oc=oc, sl=sl, blk=blk:
                           v.scalar_tensor_tensor(
                               out=t4_buf[sl].ap,
                               in0=yv[oc].ap,
                               scalar=gam_sb.ap[:, 8 * blk + oc:
                                                8 * blk + oc + 1],
                               in1=B[6].ap, op0=ALU.mult, op1=ALU.mult),
                           ins=[yv[oc], gam_sb, B[6]], out=t4_buf[sl])
                B[6].rtoks.append(tok)
                k.op("vector", lambda v, sl=sl, blk=blk, oc=oc: v.tensor_scalar(
                    out=t5_buf[sl].ap, in0=t4_buf[sl].ap,
                    scalar1=bet_sb.ap[:, 8 * blk + oc:8 * blk + oc + 1],
                    scalar2=None, op0=ALU.add),
                    ins=[t4_buf[sl], bet_sb], out=t5_buf[sl])
                k.op("scalar", lambda s, oc=oc, sl=sl: s.activation(
                    out=g1v[oc].ap, in_=t5_buf[sl].ap,
                    func=ACTF.Gelu), ins=[t5_buf[sl]], out=g1v[oc])
            tap("g10", g1_bf, c == 0 and blk == 0)
            tap("g11", g1_bf, c == 0 and blk == 1)
            tap("rs0", st_rs, c == 0 and blk == 0)
            ur2 = slice(32, 64)
            for dc in range(NDC):
                k.mm(B[4], B[4].ap[ur2, :],
                     laT_sb[:, 192 * dc + 32 * lB:192 * dc + 32 * lB + 32],
                     g1v[dc].ap, dc == 0, dc == NDC - 1,
                     ins=[laT_sb, g1v[dc]])
            tok = k.op("vector", lambda v, ur2=ur2: v.tensor_copy(
                out=u_bf.ap[ur2, :], in_=B[4].ap[ur2, :]), ins=[B[4]],
                out=u_bf)
            B[4].rtoks.append(tok)
            for oc in range(NDC):
                bank = B[2 + oc % 2]
                k.mm(bank, bank.ap[:, :],
                     lbT_sb.ap[ur2, D * blk + 128 * oc:D * blk + 128 * oc + 128],
                     u_bf.ap[ur2, :], True, False, ins=[lbT_sb, u_bf])
                for dc in range(NDC):
                    k.mm(bank, bank.ap[:, :],
                         wfull[wB][:, D * dc + 128 * oc:D * dc + 128 * oc + 128],
                         g1v[dc].ap, False, dc == NDC - 1,
                         ins=[wfull[wB], g1v[dc]])
                tok = k.op("vector", lambda v, bank=bank, oc=oc, lB=lB,
                           hc=hcurv[oc], hn=hnewv[oc]: v.scalar_tensor_tensor(
                               out=hn.ap, in0=bank.ap,
                               scalar=bcol_sb.ap[:, 8 * lB + oc:
                                                 8 * lB + oc + 1],
                               in1=hc.ap,
                               op0=ALU.add, op1=ALU.add),
                           ins=[bank, bcol_sb, hcurv[oc]], out=hnewv[oc])
                bank.rtoks.append(tok)
                k.op("scalar", lambda s, oc=oc, hn=hnewv[oc]: s.copy(
                    out=hbv[oc].ap, in_=hn.ap), ins=[hnewv[oc]], out=hbv[oc])
            tap("h1", h_sb[1 - blk % 2], c == 0 and blk == 0)
            nl2 = lB + 3
            if nl2 < 6:
                load_w(nl2, nl2 % 3)
            elif c + 1 < NCH:
                load_w(nl2 - 6, (nl2 - 6) % 3)
        hfinv = hv[1]
        for oc in range(NDC):
            sl = oc % 2
            k.dma("sync", "si_x",
                  lambda e, oc=oc, cs=cs, sl=sl: e.dma_start(
                      out=s_in[sl].ap,
                      in_=s_scr[128 * oc:128 * oc + 128, cs]), out=s_in[sl])
            k.dma("sync", "si_x",
                  lambda e, oc=oc, cs=cs, sl=sl: e.dma_start(
                      out=x_in[sl].ap,
                      in_=ap["xT"][128 * oc:128 * oc + 128, cs]), out=x_in[sl])
            k.op("scalar", lambda s, sl=sl, hf=hfinv[oc]: s.activation(
                out=sg_buf[sl].ap, in_=hf.ap,
                func=ACTF.Sigmoid), ins=[hfinv[oc]], out=sg_buf[sl])
            k.op("vector", lambda v, sl=sl, hf=hfinv[oc]: v.tensor_tensor(
                out=gg_buf[sl].ap, in0=hf.ap,
                in1=sg_buf[sl].ap, op=ALU.mult),
                ins=[hfinv[oc], sg_buf[sl]], out=gg_buf[sl])
            k.op("vector", lambda v, sl=sl: v.tensor_tensor(
                out=e2_buf[sl].ap, in0=gg_buf[sl].ap, in1=s_in[sl].ap,
                op=ALU.mult), ins=[gg_buf[sl], s_in[sl]], out=e2_buf[sl])
            k.op("vector", lambda v, sl=sl: v.tensor_tensor(
                out=o_buf[sl].ap, in0=e2_buf[sl].ap, in1=x_in[sl].ap,
                op=ALU.add), ins=[e2_buf[sl], x_in[sl]], out=o_buf[sl])
            k.dma("sync", "si_out",
                  lambda e, oc=oc, cs=cs, sl=sl: e.dma_start(
                      out=outT[128 * oc:128 * oc + 128, cs],
                      in_=o_buf[sl].ap), ins=[o_buf[sl]])

    es2.close()
    final = [("si_out", k.cnt["si_out"])]
    k.finalize(final)
    k.es.close()
    return nc


def _host_constants():
    c = {}
    c["ident"] = np.eye(128, dtype=BF)
    p = np.arange(128)
    c["diagmask"] = (p[:, None] <= p[None, :]).astype(BF)
    c["selhalf"] = (p[:, None] // 64 == np.arange(2)[None, :]).astype(BF)
    c["selpair_f"] = (np.arange(2)[:, None] == p[None, :] // 64).astype(np.float32)
    c["selrq_f"] = (np.arange(2)[:, None] == (p[None, :] % 64) // 32).astype(np.float32)
    c["ones_1x128_b"] = np.ones((1, 128), BF)
    c["nones_1x128_b"] = -np.ones((1, 128), BF)
    c["ones_1x128_f"] = np.ones((1, 128), np.float32)
    c["ones_c128_b"] = np.ones((128, 1), BF)
    c["ones_c128_f"] = np.ones((128, 1), np.float32)
    c["ones_row_b"] = np.ones((1, C), BF)
    oc3 = np.zeros((128, 3), BF)
    oc3[:, 0] = 1.0
    oc3[:, 2] = 1.0
    c["onescol3"] = oc3
    return c


def _fm_cols(v):
    n = v.shape[0]
    out = np.zeros((128, 8 * n), np.float32)
    for i in range(n):
        out[:, 8 * i:8 * i + 8] = v[i].reshape(8, 128).T
    return out


def prep_in_maps(L0_attn_x, L0_attn_Wqkv, L0_attn_bqkv, L0_attn_Wout,
                 L0_attn_bout, L0_attn_omega, L0_attn_quad_nodes,
                 L0_attn_quad_weights, L0_ffn_qweights, L0_ffn_scales,
                 L0_ffn_biases, L0_ffn_lora_a, L0_ffn_lora_b,
                 L0_ffn_ln_gamma, L0_ffn_ln_beta):
    x = np.ascontiguousarray(np.asarray(L0_attn_x, np.float32))
    Wqkv = np.ascontiguousarray(np.asarray(L0_attn_Wqkv, np.float32))
    bqkv = np.asarray(L0_attn_bqkv, np.float32)
    Wout = np.ascontiguousarray(np.asarray(L0_attn_Wout, np.float32))
    bout = np.asarray(L0_attn_bout, np.float32)
    omega = np.asarray(L0_attn_omega, np.float32)
    qnodes = np.asarray(L0_attn_quad_nodes, np.float32)
    qweights = np.asarray(L0_attn_quad_weights, np.float32)
    qint = np.ascontiguousarray(np.asarray(L0_ffn_qweights, np.int32))
    scales = np.asarray(L0_ffn_scales, np.float32)
    biases = np.asarray(L0_ffn_biases, np.float32)
    lora_a = np.asarray(L0_ffn_lora_a, np.float32)
    lora_b = np.asarray(L0_ffn_lora_b, np.float32)
    ln_g = np.asarray(L0_ffn_ln_gamma, np.float32)
    ln_b = np.asarray(L0_ffn_ln_beta, np.float32)

    omgblk = np.zeros((128, 8 * 128), np.float32)
    for i in range(8):
        for par in range(2):
            h = 2 * i + par
            O = np.transpose(omega[:, h], (1, 0, 2)).reshape(64, 64)
            omgblk[64 * par:64 * par + 64,
                   128 * i + 64 * par:128 * i + 64 * par + 64] = O
    qT = np.ascontiguousarray(qint.transpose(0, 2, 1)).reshape(6 * D, D)
    scT = np.ascontiguousarray(
        np.repeat(scales, 16, axis=2).transpose(0, 2, 1)).reshape(6 * D, D)
    laT = np.ascontiguousarray(lora_a.transpose(2, 0, 1)).reshape(D, 192)
    lbT = np.zeros((64, 3 * D), np.float32)
    for b_ in range(3):
        lbT[0:32, D * b_:D * (b_ + 1)] = lora_b[2 * b_].T
        lbT[32:64, D * b_:D * (b_ + 1)] = lora_b[2 * b_ + 1].T

    shared = {
        "wqkv": Wqkv, "wout": Wout,
        "bq_row": np.ascontiguousarray(bqkv[None, 0:D]),
        "bk_row": np.ascontiguousarray(bqkv[None, D:2 * D]),
        "bv_row": np.ascontiguousarray(bqkv[None, 2 * D:3 * D]),
        "bout_fm": np.ascontiguousarray(bout.reshape(8, 128).T),
        "omgblk": omgblk,
        "qn2": np.ascontiguousarray(qnodes.reshape(2, 1)),
        "qw2": np.ascontiguousarray(qweights.reshape(2, 1)),
        "qT": qT, "scT": scT,
        "bcol_fm": _fm_cols(biases), "gam_fm": _fm_cols(ln_g),
        "bet_fm": _fm_cols(ln_b),
        "laT": laT, "lbT": lbT,
    }
    shared.update(_host_constants())

    in_maps = []
    for b_ in range(8):
        m = dict(shared)
        m["xT"] = np.ascontiguousarray(x[b_].T)
        in_maps.append(m)
    return in_maps


def kernel(**inputs):
    in_maps = prep_in_maps(**inputs)
    nc = build_program()
    res = run_bass_kernel_spmd(nc, in_maps, core_ids=list(range(8)))
    out = np.stack([np.ascontiguousarray(np.asarray(r["outT"]).T)
                    for r in res.results])
    return np.ascontiguousarray(out)
